# version 01 (final)
# baseline (speedup 1.0000x reference)
import math
from contextlib import ExitStack

import numpy as np
import ml_dtypes

import concourse.bass as bass
import concourse.mybir as mybir
from concourse.bass_utils import run_bass_kernel_spmd

F32 = mybir.dt.float32
BF16 = mybir.dt.bfloat16
I32 = mybir.dt.int32
AF = mybir.ActivationFunctionType
ALU = mybir.AluOpType
AX = mybir.AxisListType

BF = ml_dtypes.bfloat16
RND_C = 3.0 * 2.0**22
NEG = -50.0
NBLOCKS = 4
DBG_TENSORS = set()
SAFE_MODE = False
SAFE_STAGES = set()
DBG_COPIES = []

D = 2048
DC = 16
T = 1024
NT = 8



def _tile_w(wT):
    I, O = wT.shape
    r = wT.reshape(I // 128, 128, O // 128, 128).transpose(2, 1, 0, 3)
    return np.ascontiguousarray(r).reshape(O // 128, 128, I)


def _cols(b):
    return np.ascontiguousarray(b.reshape(-1, 128).T)


def _rope_fm(pos):
    inv = (1.0 / (10000.0 ** (np.arange(0, 128, 2, dtype=np.float32) / np.float32(128.0)))).astype(np.float32)
    ang = pos[:, None].astype(np.float32) * inv[None, :]
    full = np.concatenate([ang, ang], axis=1)
    sinS = np.sin(full)
    sinS[:, :64] *= -1.0
    return (np.ascontiguousarray(np.cos(full).T.astype(np.float32)),
            np.ascontiguousarray(sinS.T.astype(np.float32)))


def _rope_tm(pos):
    inv = (1.0 / (10000.0 ** (np.arange(0, 128, 2, dtype=np.float32) / np.float32(128.0)))).astype(np.float32)
    ang = pos[:, None].astype(np.float32) * inv[None, :]
    full = np.concatenate([ang, ang], axis=1)
    cos, sinS = np.cos(full), np.sin(full)
    sinS[:, :64] *= -1.0
    nt = len(pos) // 128
    c = cos.reshape(nt, 128, 128).transpose(1, 0, 2).reshape(128, nt * 128)
    s = sinS.reshape(nt, 128, 128).transpose(1, 0, 2).reshape(128, nt * 128)
    return np.ascontiguousarray(c.astype(np.float32)), np.ascontiguousarray(s.astype(np.float32))


TERN_SPECS = [
    ("t_l0g", "L0_ffn_w_gate", 0, 1e-5, True),
    ("t_l0u", "L0_ffn_w_up", 0, 1e-5, True),
    ("t_l0d", "L0_ffn_w_down", 0, 1e-5, True),
    ("t_l2a", "L2_ffn_w1", 512, 1e-5, True),
    ("t_l2b", "L2_ffn_w2", 512, 1e-5, True),
    ("t_l2c", "L2_ffn_w3", 0, 1e-5, True),
    ("t_l3s", "L3_ffn_W_syn", 0, 1e-8, False),
    ("t_l3o", "L3_ffn_W_out", 0, 1e-8, False),
]


def _prep_host(inputs):
    f32 = np.float32
    shared = {}

    w_attn = np.asarray(inputs["L0_attn_w_attn"], f32)
    shared["l0_wqk"] = _tile_w(np.ascontiguousarray(w_attn[:, :4096]))
    shared["l0_wv"] = np.ascontiguousarray(w_attn[:, 4096:])
    b_attn = np.asarray(inputs["L0_attn_b_attn"], f32)
    shared["l0_bqk"] = _cols(b_attn[:4096])
    shared["l0_bv"] = np.ascontiguousarray(b_attn[None, 4096:])
    shared["l0_wp"] = _tile_w(np.asarray(inputs["L0_attn_w_proj"], f32))
    shared["l0_bp"] = _cols(np.asarray(inputs["L0_attn_b_proj"], f32))
    shared["l0_lnw"] = _cols(np.asarray(inputs["L0_ffn_ln_weight"], f32))

    wqkvT = np.ascontiguousarray(np.asarray(inputs["L1_attn_w_qkv"], f32).T)
    shared["l1_wqk"] = _tile_w(np.ascontiguousarray(wqkvT[:, :4096]))
    shared["l1_wv"] = np.ascontiguousarray(wqkvT[:, 4096:])
    shared["l1_wo"] = _tile_w(np.ascontiguousarray(np.asarray(inputs["L1_attn_wo"], f32).T))
    for nm in ("lambda_q1", "lambda_q2", "lambda_k1", "lambda_k2"):
        shared["l1_" + nm] = np.ascontiguousarray(np.asarray(inputs["L1_attn_" + nm], f32)[:, None])
    shared["l1_wqg"] = _tile_w(np.ascontiguousarray(np.asarray(inputs["L1_ffn_wq_gate"], np.int32).T))
    shared["l1_wqu"] = _tile_w(np.ascontiguousarray(np.asarray(inputs["L1_ffn_wq_up"], np.int32).T))
    shared["l1_wqd"] = _tile_w(np.ascontiguousarray(np.asarray(inputs["L1_ffn_wq_down"], np.int32).T))
    shared["l1_sg"] = _cols(np.asarray(inputs["L1_ffn_s_gate"], f32))
    shared["l1_su"] = _cols(np.asarray(inputs["L1_ffn_s_up"], f32))
    shared["l1_sd"] = _cols(np.asarray(inputs["L1_ffn_s_down"], f32))

    shared["l2_wq"] = np.asarray(inputs["L2_attn_Wq"], f32)
    shared["l2_wk"] = np.asarray(inputs["L2_attn_Wk"], f32)
    shared["l2_wv"] = _tile_w(np.asarray(inputs["L2_attn_Wv"], f32))
    shared["l2_wo"] = _tile_w(np.asarray(inputs["L2_attn_Wo"], f32))

    shared["l3_wq"] = _tile_w(np.ascontiguousarray(np.asarray(inputs["L3_attn_Wq"], f32).T))
    shared["l3_wk"] = np.ascontiguousarray(np.asarray(inputs["L3_attn_Wk"], f32).T)
    shared["l3_wv"] = np.ascontiguousarray(np.asarray(inputs["L3_attn_Wv"], f32).T)
    shared["l3_wo"] = _tile_w(np.ascontiguousarray(np.asarray(inputs["L3_attn_Wo"], f32).T))
    shared["l3_bsyn"] = _cols(np.asarray(inputs["L3_ffn_b_syn"], f32))
    shared["l3_bout"] = _cols(np.asarray(inputs["L3_ffn_b_out"], f32))
    shared["l3_gam"] = _cols(np.asarray(inputs["L3_ffn_gamma"], f32))
    shared["l3_bet"] = _cols(np.asarray(inputs["L3_ffn_beta"], f32))

    tern_tiled = {}
    tern_meta = {}
    for nm, key, pad, eps, clip in TERN_SPECS:
        w = np.ascontiguousarray(np.asarray(inputs[key], f32).T)
        if pad:
            w = np.ascontiguousarray(np.pad(w, ((0, 0), (0, pad))))
        tw = _tile_w(w)
        tern_tiled[nm] = tw
        tern_meta[nm] = dict(stripes=tw.shape[0] // 8, I=tw.shape[2],
                             count=float((w.shape[1] - pad) * w.shape[0]), eps=eps, clip=clip)

    x_full = np.asarray(inputs["L0_attn_x"], f32)
    l1_gamma = np.asarray(inputs["L1_attn_gamma"], f32)
    l1_beta = np.asarray(inputs["L1_attn_beta"], f32)

    consts = {}
    consts["c_ident"] = np.eye(128, dtype=BF)
    consts["c_ident_f32"] = np.eye(128, dtype=f32)
    consts["c_ones_row_bf"] = np.ones((1, 128), BF)
    consts["c_ones_col_bf"] = np.ones((128, 1), BF)
    consts["c_ones_row_f32"] = np.ones((1, 128), f32)
    consts["c_ones_col_f32"] = np.ones((128, 1), f32)
    ii, jj = np.meshgrid(np.arange(128), np.arange(128), indexing="ij")
    consts["c_diagT"] = (NEG * (ii > jj)).astype(f32)
    consts["c_diagTM"] = (NEG * (jj > ii)).astype(f32)
    consts["c_l1diagT"] = (1.0 * (ii > jj)).astype(f32)
    consts["c_l3mask"] = (ii <= jj).astype(BF)
    consts["c_iota8"] = np.ascontiguousarray(np.broadcast_to(np.arange(8, dtype=f32)[None, :], (128, 8)))
    consts["c_iota16"] = np.ascontiguousarray(np.broadcast_to(np.arange(16, dtype=f32)[None, :], (128, 16)))
    consts["c_rot"] = (ii == ((jj + 64) % 128)).astype(BF)

    in_maps = []
    for c in range(8):
        p, h = c // 2, c % 2
        m = dict(shared)
        m.update(consts)
        gtiles = [2 * j + h for j in range(NT)]
        tok = np.concatenate([np.arange(g * 128, (g + 1) * 128) for g in gtiles])
        m["x0"] = np.ascontiguousarray(x_full[p, tok, :])
        m["l1_gam_row"] = np.ascontiguousarray(l1_gamma[None, tok])
        m["l1_bet_row"] = np.ascontiguousarray(l1_beta[None, tok])
        m["rope1_cos"], m["rope1_sin"] = _rope_fm(tok)
        pos3 = np.arange(h * 1024, (h + 1) * 1024)
        m["rope3_cos"], m["rope3_sin"] = _rope_fm(pos3)
        m["rope3c_tm"], m["rope3s_tm"] = _rope_tm(pos3)
        tril_s = (ii > jj).astype(f32)
        triu_s = (jj > ii).astype(f32)
        ones_t = np.ones((128, 128), f32)
        zer_t = np.zeros((128, 128), f32)
        if h == 0:
            m["l0_MeT"], m["l0_MoT"] = NEG * tril_s, NEG * ones_t
            m["l0_MeM"], m["l0_MoM"] = NEG * triu_s, NEG * ones_t
            m["l1_Me1T"], m["l1_Mo1T"] = 1.0 * tril_s, 1.0 * ones_t
        else:
            m["l0_MeT"], m["l0_MoT"] = zer_t, NEG * tril_s
            m["l0_MeM"], m["l0_MoM"] = zer_t, NEG * triu_s
            m["l1_Me1T"], m["l1_Mo1T"] = zer_t, 1.0 * tril_s
        ridx = np.empty((128, 8), np.int32)
        for i in range(8):
            g = 8 * h + i
            ridx[:, i] = (g % 2) * 1024 + (g // 2) * 128 + np.arange(128)
        m["resh_idx"] = ridx
        m["c_role0"] = np.full((128, 1), 1.0 if h == 0 else 0.0, f32)
        m["c_role1"] = np.full((128, 1), 1.0 if h == 1 else 0.0, f32)
        for nm in tern_tiled:
            st = tern_meta[nm]["stripes"]
            m[nm] = np.ascontiguousarray(tern_tiled[nm][c * st:(c + 1) * st])
        in_maps.append(m)
    return in_maps, tern_meta




_GUARD_OPS = {
    "tensor_copy", "tensor_tensor", "tensor_add", "tensor_sub", "tensor_mul",
    "tensor_max", "tensor_scalar", "tensor_single_scalar", "tensor_scalar_add",
    "tensor_scalar_sub", "tensor_scalar_mul", "tensor_scalar_max", "tensor_scalar_min",
    "tensor_reduce", "reduce_max", "reduce_sum", "scalar_tensor_tensor",
    "tensor_tensor_scan", "select", "copy_predicated", "reciprocal", "bn_stats",
    "bn_aggr", "max", "max_index", "match_replace", "memset", "activation",
    "sign", "square", "sqrt", "mul", "copy", "add", "tensor_relu", "transpose",
}


def _ap_span(a):
    try:
        nm = a.tensor.name
        off = a.offset
        hi = off + 1
        for (st, cnt) in a.ap:
            hi += abs(st) * (cnt - 1)
        return (nm, off, hi)
    except Exception:
        return None


def _spans(args, kw):
    outs, ins = [], []
    first = True
    for a in args:
        sp = _ap_span(a) if hasattr(a, "tensor") else None
        if sp is not None:
            (outs if first else ins).append(sp)
            first = False
        elif first:
            pass
    for key, a in kw.items():
        sp = _ap_span(a) if hasattr(a, "tensor") else None
        if sp is None:
            continue
        (outs if key == "accum_out" else ins).append(sp)
    return outs, ins


def _ovl(a, b):
    for (n1, l1, h1) in a:
        for (n2, l2, h2) in b:
            if n1 == n2 and l1 < h2 and l2 < h1:
                return True
    return False


class Guard:
    def __init__(self, k, handle, ename):
        self._k, self._h, self._e = k, handle, ename

    def __getattr__(self, name):
        real = getattr(self._h, name)
        if name not in _GUARD_OPS:
            return real
        k, e = self._k, self._e

        def call(*args, **kw):
            outs, ins = _spans(args, kw)
            hist = k._hist[e]
            for rec in hist:
                pinst, pouts, pins = rec[0], rec[1], rec[2]
                if _ovl(ins, pouts) or _ovl(outs, pouts) or _ovl(outs, pins):
                    known = k.incmap.get(id(pinst))
                    if known is None:
                        hz = k.hz[e]
                        k.inc(pinst, hz)
                        known = (hz, hz.cnt)
                    self._h.wait_ge(known[0].h, known[1])
                    break
            inst = real(*args, **kw)
            k._hist[e] = [[inst, outs, ins, None]] + hist[:1]
            return inst
        return call


class Sem:
    def __init__(self, h, step=1):
        self.h, self.step, self.cnt = h, step, 0


class K:

    def sec(self, eng):
        dec0 = {"t": self.block.tensor, "v": self.block.vector, "a": self.block.scalar,
                "g": self.block.gpsimd, "s": self.block.sync}[eng]
        if eng in ("v", "a", "g"):
            def dec(f, dec0=dec0, eng=eng):
                return dec0(lambda h: f(Guard(self, h, eng)))
        else:
            dec = dec0
        if not (SAFE_MODE or getattr(self, "_safe_now", False)):
            return dec

        def wrap(f):
            def g(handle):
                for sem in (self.sT, self.sV, self.sA, self.sG, self.cC,
                            self.dS, self.dG, self.dW, self.dK):
                    if sem.cnt:
                        handle.wait_ge(sem.h, sem.cnt)
                return f(handle)
            return dec(g)
        return wrap

    def inc(self, inst, sem):
        inst.then_inc(sem.h, sem.step)
        sem.cnt += sem.step
        self.incmap[id(inst)] = (sem, sem.cnt)
        return sem.cnt

    def constp(self, val, p=128):
        col = self.constmap[val]
        return self.cst[0:p, col:col + 1]


def build_program(nblocks, tern_meta):
    nc = bass.Bass("TRN2", target_bir_lowering=False, debug=False)
    k = K()
    k.nc = nc
    ap = {}
    k.ap = ap

    def In(name, shape, dt):
        ap[name] = nc.dram_tensor(name, list(shape), dt, kind="ExternalInput").ap()

    In("x0", (T, D), F32)
    In("l0_wqk", (32, 128, D), F32); In("l0_wv", (D, D), F32)
    In("l0_bqk", (128, 32), F32); In("l0_bv", (1, D), F32)
    In("l0_wp", (16, 128, D), F32); In("l0_bp", (128, 16), F32)
    In("l0_lnw", (128, 64), F32)
    In("l1_wqk", (32, 128, D), F32); In("l1_wv", (D, D), F32); In("l1_wo", (16, 128, D), F32)
    for nm in ("lambda_q1", "lambda_q2", "lambda_k1", "lambda_k2"):
        In("l1_" + nm, (128, 1), F32)
    In("l1_gam_row", (1, T), F32); In("l1_bet_row", (1, T), F32)
    In("l1_wqg", (44, 128, D), I32); In("l1_wqu", (44, 128, D), I32)
    In("l1_wqd", (16, 128, 5632), I32)
    In("l1_sg", (128, 44), F32); In("l1_su", (128, 44), F32); In("l1_sd", (128, 16), F32)
    In("l2_wq", (D, D), F32); In("l2_wk", (D, D), F32)
    In("l2_wv", (16, 128, D), F32); In("l2_wo", (16, 128, D), F32)
    In("l3_wq", (16, 128, D), F32); In("l3_wk", (D, 512), F32); In("l3_wv", (D, 512), F32)
    In("l3_wo", (16, 128, D), F32)
    In("l3_bsyn", (128, 64), F32); In("l3_bout", (128, 16), F32)
    In("l3_gam", (128, 16), F32); In("l3_bet", (128, 16), F32)
    In("c_ident", (128, 128), BF16)
    In("c_ident_f32", (128, 128), F32)
    In("c_ones_row_bf", (1, 128), BF16); In("c_ones_col_bf", (128, 1), BF16)
    In("c_ones_row_f32", (1, 128), F32); In("c_ones_col_f32", (128, 1), F32)
    In("c_diagT", (128, 128), F32); In("c_diagTM", (128, 128), F32)
    In("c_l1diagT", (128, 128), F32); In("c_l3mask", (128, 128), BF16)
    In("c_iota8", (128, 8), F32)
    In("c_iota16", (128, 16), F32)
    In("c_rot", (128, 128), BF16)
    In("rope1_cos", (128, T), F32); In("rope1_sin", (128, T), F32)
    In("rope3_cos", (128, T), F32); In("rope3_sin", (128, T), F32)
    In("rope3c_tm", (128, T), F32); In("rope3s_tm", (128, T), F32)
    for nm in ("l0_MeT", "l0_MoT", "l0_MeM", "l0_MoM", "l1_Me1T", "l1_Mo1T"):
        In(nm, (128, 128), F32)
    In("resh_idx", (128, 8), I32)
    In("c_role0", (128, 1), F32); In("c_role1", (128, 1), F32)
    for nm, meta in tern_meta.items():
        In(nm, (meta["stripes"], 128, meta["I"]), F32)

    ap["xout"] = nc.dram_tensor("xout", [T, D], F32, kind="ExternalOutput").ap()

    def Dr(name, shape, dt):
        kind = "ExternalOutput" if name in DBG_TENSORS else "Internal"
        ap[name] = nc.dram_tensor(name, list(shape), dt, kind=kind).ap()

    Dr("kv0k_in", (DC, 128, T), BF16); Dr("kv0k_out", (2, DC, 128, T), BF16)
    Dr("kv0v_in", (16, NT, 128, 128), BF16); Dr("kv0v_out", (2, 16, NT, 128, 128), BF16)
    Dr("kv1k_in", (DC, 128, T), BF16); Dr("kv1k_out", (2, DC, 128, T), BF16)
    Dr("kv1v_in", (16, NT, 128, 128), BF16); Dr("kv1v_out", (2, 16, NT, 128, 128), BF16)
    Dr("kv2q_in", (NT, 128, D), BF16)
    Dr("s2_in", (16, 128, D), F32); Dr("s2_out", (16, 128, D), F32)

    Dr("resh_in", (NT, 128, D), F32); Dr("resh_out", (2, NT, 128, D), F32)
    Dr("st3_in", (128, 4 * 128 + 4), F32); Dr("st3_out", (128, 4 * 128 + 4), F32)
    for nm, meta in tern_meta.items():
        Dr(nm + "_bin", (meta["stripes"], 128, meta["I"]), BF16)
        Dr(nm + "_bout", (8, meta["stripes"], 128, meta["I"]), BF16)
    Dr("am_in", (1, 8), F32); Dr("am_out", (1, 8), F32)
    Dr("h_tm_hbm", (NT, 128, 8192), F32)
    Dr("hT_hbm", (64, 128, T), BF16)
    Dr("q_hbm", (DC, 128, T), BF16)
    if "hid3" in DBG_TENSORS:
        ap["dbg_hid3"] = nc.dram_tensor("dbg_hid3", [NT, 128, 8192], BF16,
                                        kind="ExternalOutput").ap()
    if "nd3" in DBG_TENSORS:
        ap["dbg_num3"] = nc.dram_tensor("dbg_num3", [16, 128, T], F32,
                                        kind="ExternalOutput").ap()
        ap["dbg_den3"] = nc.dram_tensor("dbg_den3", [16, 1, T], F32,
                                        kind="ExternalOutput").ap()
    if "o2" in DBG_TENSORS:
        ap["dbg_o2"] = nc.dram_tensor("dbg_o2", [16, 128, T], BF16,
                                      kind="ExternalOutput").ap()
    if "p1b1" in DBG_TENSORS:
        ap["dbg_p1b1"] = nc.dram_tensor("dbg_p1b1", [16, 128, T], BF16,
                                        kind="ExternalOutput").ap()
        ap["dbg_qe"] = nc.dram_tensor("dbg_qe", [16, 128, T], BF16,
                                      kind="ExternalOutput").ap()
        ap["dbg_p1pre"] = nc.dram_tensor("dbg_p1pre", [16, 128, T], BF16,
                                         kind="ExternalOutput").ap()
    if "row" in DBG_TENSORS:
        ap["dbg_row"] = nc.dram_tensor("dbg_row", [128, 256 + 96 + 1024], F32,
                                       kind="ExternalOutput").ap()
    if "ln" in DBG_TENSORS:
        ap["dbg_ln"] = nc.dram_tensor("dbg_ln", [128, 2048 + 96], F32, kind="ExternalOutput").ap()
    for nm in DBG_COPIES:
        shp = {"q_hbm": (DC, 128, T), "qx_hbm": (DC, 128, T), "g_hbm": (DC, 128, T),
               "hT_hbm": (64, 128, T), "h_tm_hbm": (NT, 128, 8192),
               "kv0k_in": (DC, 128, T), "kv0v_in": (16, NT, 128, 128),
               "kv1k_in": (DC, 128, T), "kv1v_in": (16, NT, 128, 128),
               "kv2q_in": (NT, 128, D), "st3_in": (128, 516), "st3_out": (128, 516),
               "s2_in": (16, 128, D), "s2_out": (16, 128, D),
               "t_l0g_bin": (8, 128, D), "t_l0g_bout": (8, 8, 128, D)}[nm]
        dt = F32 if nm in ("h_tm_hbm", "s2_in", "s2_out", "st3_in", "st3_out") else BF16
        ap["dbg_" + nm] = nc.dram_tensor("dbg_" + nm, list(shp), dt, kind="ExternalOutput").ap()
    Dr("g_hbm", (DC, 128, T), BF16)
    Dr("qx_hbm", (DC, 128, T), BF16)

    with ExitStack() as ctx:
        e = ctx.enter_context
        k.x_tm = e(nc.sbuf_tensor("x_tm", [128, NT * D], F32))
        k.P1 = e(nc.sbuf_tensor("P1", [128, DC * T], BF16))
        k.SCR = e(nc.sbuf_tensor("SCR", [128, 8192], F32))
        k.WSa = e(nc.sbuf_tensor("WSa", [128, 8192], BF16))
        k.WSb = e(nc.sbuf_tensor("WSb", [128, 8192], BF16))
        k.ATN = e(nc.sbuf_tensor("ATN", [128, 8192], BF16))
        k.hsq = e(nc.sbuf_tensor("hsq", [128, 1024], BF16))
        k.lnw_eff = e(nc.sbuf_tensor("lnw_eff", [128, 64], F32))
        k.ilnw2bf = e(nc.sbuf_tensor("ilnw2bf", [128, 64], BF16))
        k.bias = e(nc.sbuf_tensor("bias", [128, 160], F32))
        k.stat = e(nc.sbuf_tensor("stat", [128, 96], F32))
        k.rowA = e(nc.sbuf_tensor("rowA", [1, T], F32))
        k.rowB = e(nc.sbuf_tensor("rowB", [1, T], F32))
        k.rowC = e(nc.sbuf_tensor("rowC", [1, 256], F32))
        k.bbf = e(nc.sbuf_tensor("bbf", [1, 2048], BF16))
        k.bcA = e(nc.sbuf_tensor("bcA", [128, T], F32))
        k.bcB = e(nc.sbuf_tensor("bcB", [128, T], F32))
        k.cst = e(nc.sbuf_tensor("cst", [128, 560], F32))
        k.msk = e(nc.sbuf_tensor("msk", [128, 768], F32))
        k.ridx = e(nc.sbuf_tensor("ridx", [128, 8], I32))
        k.cbf = e(nc.sbuf_tensor("cbf", [128, 768], BF16))
        k.cident_f = e(nc.sbuf_tensor("cident_f", [128, 128], F32))
        k.rks_sb = e(nc.sbuf_tensor("rks_sb", [1, 2], I32))
        k.rkp_sb = e(nc.sbuf_tensor("rkp_sb", [1, 2], I32))
        k.ps_a = e(nc.psum_tensor("ps_a", [128, 1024], F32))
        k.ps_b = e(nc.psum_tensor("ps_b", [128, 1024], F32))
        k.ps_c = e(nc.psum_tensor("ps_c", [128, 512], F32))
        k.ps_d = e(nc.psum_tensor("ps_d", [128, 512], F32))
        k.ps_t = e(nc.psum_tensor("ps_t", [128, 1024], BF16))
        k.block = e(nc.Block())
        k.sT = Sem(e(nc.semaphore("sT")))
        k.sV = Sem(e(nc.semaphore("sV")))
        k.sA = Sem(e(nc.semaphore("sA")))
        k.sG = Sem(e(nc.semaphore("sG")))
        k.dS = Sem(e(nc.semaphore("dS")), 16)
        k.dG = Sem(e(nc.semaphore("dG")), 16)
        k.dW = Sem(e(nc.semaphore("dW")), 16)
        k.dK = Sem(e(nc.semaphore("dK")), 16)
        k.cC = Sem(e(nc.semaphore("cC")))
        k.hz = {"v": Sem(e(nc.semaphore("hzV"))), "a": Sem(e(nc.semaphore("hzA"))),
                "g": Sem(e(nc.semaphore("hzG")))}
        k._hist = {"v": [], "a": [], "g": []}
        k.incmap = {}

        k.C_ident = k.cbf[:, 0:128]
        k.C_ones_row = k.cbf[:1, 128:256]
        k.C_ones_col = k.cbf[:, 256:257]
        k.C_l3mask = k.cbf[:, 384:512]
        k.C_diagT = k.cst[:, 0:128]
        k.C_diagTM = k.cst[:, 128:256]
        k.C_l1diagT = k.cst[:, 256:384]
        k.C_iota8 = k.cst[:, 384:392]
        k.C_iota16 = k.cst[:, 392:408]
        k.C_rot = k.cbf[:, 512:640]
        k.M_l0MeT = k.msk[:, 0:128]
        k.M_l0MoT = k.msk[:, 128:256]
        k.M_l0MeM = k.msk[:, 256:384]
        k.M_l0MoM = k.msk[:, 384:512]
        k.M_l1Me1T = k.msk[:, 512:640]
        k.M_l1Mo1T = k.msk[:, 640:768]
        k.C_onesrow_f = k.cst[:1, 416:544]
        k.C_onescol_f = k.cst[:, 544:545]
        k.C_role0 = k.cst[:, 545:546]
        k.C_role1 = k.cst[:, 546:547]

        emit_all(k, nblocks, tern_meta)
    return nc



ST_MU, ST_SQ, ST_V1, ST_V2, ST_STD, ST_R, ST_BLN, ST_M0 = 0, 1, 2, 3, 4, 5, 6, 7
ST_T0, ST_T1, ST_T2, ST_T3 = 8, 9, 10, 11
ST_INVS0, ST_R3, ST_INVS3, ST_M = 16, 24, 32, 40
ST_TERN = 48
ST_ABS = 56
ST_Z3 = 64
ST_B = 72
TERN_ORDER = ["t_l0g", "t_l0u", "t_l0d", "t_l2a", "t_l2b", "t_l2c", "t_l3s", "t_l3o"]
SC_L1 = 0.08838834764831845


def _w(eng, sem, val=None):
    eng.wait_ge(sem.h, sem.cnt if val is None else val)


def _startup_loads(k):
    ap = k.ap

    @k.sec("s")
    def _(s):
        for j in range(NT):
            k.inc(s.dma_start(out=k.x_tm[:, j * D:(j + 1) * D],
                              in_=ap["x0"][j * 128:(j + 1) * 128, :]), k.dS)
        for dst, src in ((k.cbf[:, 0:128], "c_ident"), (k.cbf[:1, 128:256], "c_ones_row_bf"),
                         (k.cbf[:, 256:257], "c_ones_col_bf"), (k.cbf[:, 384:512], "c_l3mask"),
                         (k.cst[:, 0:128], "c_diagT"), (k.cst[:, 128:256], "c_diagTM"),
                         (k.cst[:, 256:384], "c_l1diagT"), (k.cst[:, 384:392], "c_iota8"),
                         (k.msk[:, 0:128], "l0_MeT"), (k.msk[:, 128:256], "l0_MoT"),
                         (k.msk[:, 256:384], "l0_MeM"), (k.msk[:, 384:512], "l0_MoM"),
                         (k.msk[:, 512:640], "l1_Me1T"), (k.msk[:, 640:768], "l1_Mo1T"),
                         (k.ridx[:, :], "resh_idx"), (k.cst[:1, 416:544], "c_ones_row_f32"),
                         (k.cst[:, 544:545], "c_ones_col_f32"), (k.cst[:, 545:546], "c_role0"),
                         (k.cst[:, 546:547], "c_role1"), (k.cident_f[:, :], "c_ident_f32"),
                         (k.cst[:, 392:408], "c_iota16"), (k.cbf[:, 512:640], "c_rot")):
            k.inc(s.dma_start(out=dst, in_=ap[src]), k.dS)
    k.x_ready = k.dS.cnt
    k.constmap = {1e-5: 547, 1e-6: 548, 1.0 / 8192.0: 549,
                  SC_L1: 550, 0.125: 551}

    @k.sec("v")
    def _(v):
        for val, col in k.constmap.items():
            inst = v.memset(k.cst[:, col:col + 1], val)
        k.inc(inst, k.sV)


def _pieces(I):
    out, o = [], 0
    while o < I:
        w = min(2048, I - o)
        out.append((o, w))
        o += w
    return out


def _tern_phase1(k, tern_meta):
    ap = k.ap

    @k.sec("v")
    def _(v):
        k.inc(v.memset(k.stat[:, ST_ABS:ST_ABS + 8], 0.0), k.sV)

    for mi, nm in enumerate(TERN_ORDER):
        meta = tern_meta[nm]
        for st_i in range(meta["stripes"]):
            for (o, wdt) in _pieces(meta["I"]):
                @k.sec("g")
                def _(g, nm=nm, st_i=st_i, o=o, wdt=wdt):
                    _w(g, k.sV)
                    k.inc(g.dma_start(out=k.SCR[:, 0:wdt], in_=ap[nm][st_i, :, o:o + wdt]), k.dG)

                @k.sec("a")
                def _(a, wdt=wdt):
                    _w(a, k.dG)
                    k.inc(a.activation(k.SCR[:, 4096:4096 + wdt], k.SCR[:, 0:wdt], AF.Abs,
                                       accum_out=k.stat[:, ST_T0:ST_T0 + 1]), k.sA)

                @k.sec("v")
                def _(v, mi=mi):
                    _w(v, k.sA)
                    acc = k.stat[:, ST_ABS + mi:ST_ABS + mi + 1]
                    k.inc(v.tensor_add(acc, acc, k.stat[:, ST_T0:ST_T0 + 1]), k.sV)

    vv = k.sV.cnt

    @k.sec("t")
    def _(t, vv=vv):
        t.wait_ge(k.sV.h, vv)
        for mi in range(8):
            inst = t.matmul(k.ps_c[0:1, mi:mi + 1], k.stat[:, ST_ABS + mi:ST_ABS + mi + 1],
                            k.C_onescol_f, start=True, stop=True)
        k.inc(inst, k.sT)

    @k.sec("v")
    def _(v):
        _w(v, k.sT)
        k.inc(v.tensor_copy(k.rowA[0:1, 8:16], k.ps_c[0:1, 0:8]), k.sV)

    @k.sec("g")
    def _(g):
        _w(g, k.sV)
        k.inc(g.dma_start(out=ap["am_in"], in_=k.rowA[0:1, 8:16]), k.dG)
        _w(g, k.dG)
        k.inc(g.collective_compute("AllReduce", ALU.add, replica_groups=[list(range(8))],
                                   ins=[ap["am_in"]], outs=[ap["am_out"]]), k.cC)
        _w(g, k.cC)
        k.inc(g.dma_start(out=k.rowA[0:1, 8:16], in_=ap["am_out"]), k.dG)

    @k.sec("v")
    def _(v):
        _w(v, k.dG)
        for mi, nm in enumerate(TERN_ORDER):
            meta = tern_meta[nm]
            c = k.rowA[0:1, 8 + mi:9 + mi]
            m = k.rowC[0:1, mi:mi + 1]
            v.tensor_scalar(m, c, 1.0 / meta["count"], None, ALU.mult)
            if meta["clip"]:
                v.tensor_scalar(m, m, meta["eps"], None, ALU.max)
                v.tensor_copy(k.rowB[0:1, mi:mi + 1], m)
                v.reciprocal(k.rowB[0:1, 8 + mi:9 + mi], m)
            else:
                v.tensor_copy(k.rowB[0:1, mi:mi + 1], m)
                v.tensor_scalar(m, m, meta["eps"], None, ALU.add)
                v.reciprocal(k.rowB[0:1, 8 + mi:9 + mi], m)
        k.inc(v.tensor_copy(k.rowA[0:1, 16:24], k.rowB[0:1, 8:16]), k.sV)
    vv = k.sV.cnt

    @k.sec("t")
    def _(t, vv=vv):
        t.wait_ge(k.sV.h, vv)
        t.matmul(k.ps_c[:, 0:8], k.C_onesrow_f, k.rowB[0:1, 0:8], start=True, stop=True)
        inst = t.matmul(k.ps_c[:, 8:16], k.C_onesrow_f, k.rowA[0:1, 16:24], start=True, stop=True)
        k.inc(inst, k.sT)

    @k.sec("v")
    def _(v):
        _w(v, k.sT)
        v.tensor_copy(k.stat[:, ST_TERN:ST_TERN + 8], k.ps_c[:, 0:8])
        k.inc(v.tensor_copy(k.stat[:, ST_ABS:ST_ABS + 8], k.ps_c[:, 8:16]), k.sV)
    k.tern_scales_ready = k.sV.cnt


def _tern_phase2(k, tern_meta, names):
    ap = k.ap
    k.tern_ready = dict(getattr(k, "tern_ready", {}))
    for nm in names:
        meta = tern_meta[nm]
        mi = TERN_ORDER.index(nm)
        for st_i in range(meta["stripes"]):
            for (o, wdt) in _pieces(meta["I"]):
                @k.sec("g")
                def _(g, nm=nm, st_i=st_i, o=o, wdt=wdt):
                    _w(g, k.sV)
                    _w(g, k.sT)
                    if st_i == 0 and o == 0:
                        _w(g, k.cC)
                    k.inc(g.dma_start(out=k.SCR[:, 0:wdt], in_=ap[nm][st_i, :, o:o + wdt]), k.dG)

                @k.sec("v")
                def _(v, mi=mi, wdt=wdt):
                    _w(v, k.dG)
                    sw = k.stat[:, ST_ABS + mi:ST_ABS + mi + 1]
                    v.tensor_scalar(k.SCR[:, 4096:4096 + wdt], k.SCR[:, 0:wdt], sw,
                                    1.4999999, ALU.mult, ALU.min)
                    v.tensor_scalar(k.SCR[:, 4096:4096 + wdt], k.SCR[:, 4096:4096 + wdt],
                                    -1.4999999, RND_C, ALU.max, ALU.add)
                    k.inc(v.tensor_scalar(k.ATN[:, 0:wdt], k.SCR[:, 4096:4096 + wdt],
                                          RND_C, None, ALU.subtract), k.sV)

                @k.sec("g")
                def _(g, nm=nm, st_i=st_i, o=o, wdt=wdt):
                    _w(g, k.sV)
                    k.inc(g.dma_start(out=ap[nm + "_bin"][st_i, :, o:o + wdt],
                                      in_=k.ATN[:, 0:wdt]), k.dG)

        @k.sec("g")
        def _(g, nm=nm):
            _w(g, k.dG)
            k.inc(g.collective_compute("AllGather", ALU.bypass, replica_groups=[list(range(8))],
                                       ins=[ap[nm + "_bin"]], outs=[ap[nm + "_bout"]]), k.cC)
        k.tern_ready[nm] = k.cC.cnt


def _tern_stripe_ap(k, nm, oc):
    per = k.tern_meta[nm]["stripes"]
    return k.ap[nm + "_bout"][oc // per, oc % per, :, :]


def _ln_quant(k, want_qx, save_xn=False):
    ap = k.ap
    st = k.stat
    for j in range(NT):
        xt = k.x_tm[:, j * D:(j + 1) * D]

        @k.sec("v")
        def _(v, xt=xt, j=j):
            if j == 0:
                v.wait_ge(k.dS.h, k.x_ready)
                _w(v, k.dG)
                _w(v, k.sT)
            _w(v, k.sA)
            k.inc(v.tensor_scalar(k.SCR[:, 0:2048], xt, 1.0 / D, 0.0, ALU.mult, ALU.add,
                                  accum_out=st[:, ST_MU:ST_MU + 1]), k.sV)

        @k.sec("a")
        def _(a, xt=xt):
            _w(a, k.sV)
            k.inc(a.activation(k.SCR[:, 2048:4096], xt, AF.Square,
                               accum_out=st[:, ST_SQ:ST_SQ + 1]), k.sA)

        @k.sec("v")
        def _(v):
            _w(v, k.sA)
            v.tensor_scalar(st[:, ST_V1:ST_V1 + 1], st[:, ST_SQ:ST_SQ + 1], 1.0 / D, None, ALU.mult)
            v.tensor_mul(st[:, ST_V2:ST_V2 + 1], st[:, ST_MU:ST_MU + 1], st[:, ST_MU:ST_MU + 1])
            k.inc(v.tensor_sub(st[:, ST_V1:ST_V1 + 1], st[:, ST_V1:ST_V1 + 1],
                               st[:, ST_V2:ST_V2 + 1]), k.sV)

        @k.sec("a")
        def _(a):
            _w(a, k.sV)
            k.inc(a.activation(st[:, ST_STD:ST_STD + 1], st[:, ST_V1:ST_V1 + 1], AF.Sqrt,
                               bias=k.constp(1e-5)), k.sA)

        @k.sec("v")
        def _(v):
            _w(v, k.sA)
            v.reciprocal(st[:, ST_R:ST_R + 1], st[:, ST_STD:ST_STD + 1])
            v.tensor_mul(st[:, ST_BLN:ST_BLN + 1], st[:, ST_MU:ST_MU + 1], st[:, ST_R:ST_R + 1])
            k.inc(v.tensor_scalar(st[:, ST_BLN:ST_BLN + 1], st[:, ST_BLN:ST_BLN + 1],
                                  -1.0, None, ALU.mult), k.sV)

        @k.sec("v")
        def _(v, xt=xt):
            k.inc(v.tensor_scalar(k.SCR[:, 0:2048], xt, st[:, ST_MU:ST_MU + 1],
                                  st[:, ST_R:ST_R + 1], ALU.subtract, ALU.mult), k.sV)

        @k.sec("a")
        def _(a):
            _w(a, k.sV)
            k.inc(a.activation(k.stat[:, ST_T3:ST_T3 + 1], k.stat[:, ST_R:ST_R + 1],
                               AF.Identity), k.sA)

        @k.sec("v")
        def _(v, j=j):
            _w(v, k.sA)
            v.tensor_reduce(st[:, ST_M0:ST_M0 + 1], k.SCR[:, 0:2048], AX.X, ALU.max,
                            apply_absolute_value=True)
            v.tensor_scalar(st[:, ST_M0:ST_M0 + 1], st[:, ST_M0:ST_M0 + 1], 1e-5, None, ALU.max)
            v.reciprocal(st[:, ST_T0:ST_T0 + 1], st[:, ST_M0:ST_M0 + 1])
            v.tensor_scalar(st[:, ST_T0:ST_T0 + 1], st[:, ST_T0:ST_T0 + 1], 127.0, None, ALU.mult)
            inst = v.tensor_scalar(st[:, ST_INVS0 + j:ST_INVS0 + j + 1], st[:, ST_M0:ST_M0 + 1],
                                   1.0 / 127.0, None, ALU.mult)
            if want_qx:
                v.tensor_scalar(k.SCR[:, 2048:4096], k.SCR[:, 0:2048], st[:, ST_T0:ST_T0 + 1],
                                RND_C, ALU.mult, ALU.add)
                inst = v.tensor_scalar(k.ATN[:, 0:2048], k.SCR[:, 2048:4096],
                                       RND_C, None, ALU.subtract)
            k.inc(inst, k.sV)

        if j == 0 and "ln" in DBG_TENSORS:
            @k.sec("s")
            def _(s):
                _w(s, k.sV)
                k.inc(s.dma_start(out=k.ap["dbg_ln"][:, 0:2048], in_=k.SCR[:, 0:2048]), k.dS)
                k.inc(s.dma_start(out=k.ap["dbg_ln"][:, 2048:2144], in_=k.stat[:, 0:96]), k.dS)

        @k.sec("a")
        def _(a):
            _w(a, k.sV)
            k.inc(a.activation(k.ATN[:, 2048:4096], k.SCR[:, 0:2048], AF.Copy), k.sA)

        srcs = [(2048, "xn")] + ([(0, "qx")] if want_qx else [])
        if save_xn:
            srcs = [(2048, "xn"), (2048, "qx")]
        for off, kind in srcs:
            for rnd in range(2):
                @k.sec("t")
                def _(t, off=off, rnd=rnd):
                    _w(t, k.sA)
                    _w(t, k.sV)
                    for i in range(8):
                        kc = rnd * 8 + i
                        inst = t.transpose(k.ps_t[:, i * 128:(i + 1) * 128],
                                           k.ATN[:, off + kc * 128:off + (kc + 1) * 128], k.C_ident)
                    k.inc(inst, k.sT)

                @k.sec("v")
                def _(v, rnd=rnd, j=j, kind=kind):
                    _w(v, k.sT)
                    if kind == "qx" and j % 2 == 0:
                        _w(v, k.dS)
                    if kind == "xn":
                        dst = k.P1[:, :].rearrange("p (c t) -> p c t", c=DC)[
                            :, rnd * 8:(rnd + 1) * 8, j * 128:(j + 1) * 128]
                    else:
                        dst = k.ATN[:, 4096:8192].rearrange("p (c t) -> p c t", c=DC)[
                            :, rnd * 8:(rnd + 1) * 8, (j % 2) * 128:(j % 2) * 128 + 128]
                    k.inc(v.tensor_copy(dst, k.ps_t[:, 0:1024].rearrange("p (c t) -> p c t", c=8)),
                          k.sV)

        if (want_qx or save_xn) and j % 2 == 1:
            @k.sec("s")
            def _(s, j=j):
                _w(s, k.sV)
                if j == 1:
                    _w(s, k.dS)
                for kc in range(DC):
                    k.inc(s.dma_start(out=ap["qx_hbm"][kc, :, (j - 1) * 128:(j + 1) * 128],
                                      in_=k.ATN[:, 4096 + kc * 256:4096 + (kc + 1) * 256]), k.dS)
    k.ln_vdone = k.sV.cnt
    k.qx_dS = k.dS.cnt


def _cols_to_row(k, col0, n, row_dst):
    for grp in range(0, n, 4):
        cnt = min(4, n - grp)

        @k.sec("t")
        def _(t, grp=grp, cnt=cnt):
            _w(t, k.sV)
            for i in range(cnt):
                inst = t.transpose(k.ps_c[0:1, i * 128:(i + 1) * 128],
                                   k.stat[:, col0 + grp + i:col0 + grp + i + 1], k.cident_f[:, :])
            k.inc(inst, k.sT)

        @k.sec("v")
        def _(v, grp=grp, cnt=cnt):
            _w(v, k.sT)
            k.inc(v.tensor_copy(row_dst[0:1, grp * 128:(grp + cnt) * 128],
                                k.ps_c[0:1, 0:cnt * 128]), k.sV)


def _bcast_row(k, row_ap, dst_bc, n=T, scale_col=None):
    vv = k.sV.cnt

    @k.sec("t")
    def _(t, vv=vv, n=n):
        t.wait_ge(k.sV.h, vv)
        _w(t, k.sA)
        for n0 in range(0, n, 512):
            inst = t.matmul(k.ps_a[:, n0:n0 + 512], k.C_onesrow_f,
                            row_ap[0:1, n0:n0 + 512], start=True, stop=True)
        k.inc(inst, k.sT)

    @k.sec("v")
    def _(v, n=n):
        _w(v, k.sT)
        if scale_col is not None:
            inst = v.tensor_scalar(dst_bc[:, 0:n], k.ps_a[:, 0:n], scale_col, None, ALU.mult)
        else:
            inst = v.tensor_copy(dst_bc[:, 0:n], k.ps_a[:, 0:n])
        k.inc(inst, k.sV)


def _proj_fm(k, n_oc, I, stripe_ap, rhs_ap, evac, queue="g", pre=(), slots=None):
    KC = I // 128
    slots = slots if slots is not None else [k.WSa, k.WSb]
    psums = [k.ps_a, k.ps_b]
    pfree, wfree = [None, None], [None, None]
    dsem = k.dW if queue == "g" else k.dK
    qeng = "g" if queue == "g" else "a"
    base_stage = getattr(k, "_cur_stage", "")
    for idx in range(n_oc):
        sl = idx % 2

        _stage(k, base_stage + "_w")
        @k.sec(qeng)
        def _(eng, idx=idx, sl=sl):
            if idx < 2:
                for (sm, v) in pre:
                    eng.wait_ge(sm.h, v)
            if wfree[sl] is not None:
                eng.wait_ge(wfree[sl][0].h, wfree[sl][1])
            k.inc(eng.dma_start(out=slots[sl][:, 0:I], in_=stripe_ap(idx)), dsem)
        dv = dsem.cnt

        _stage(k, base_stage + "_mm")
        @k.sec("t")
        def _(t, idx=idx, sl=sl, dv=dv):
            t.wait_ge(dsem.h, dv)
            if idx < 2:
                for (sm, v) in pre:
                    t.wait_ge(sm.h, v)
            if pfree[sl] is not None:
                t.wait_ge(pfree[sl][0].h, pfree[sl][1])
            for kc in range(KC):
                lhsT = slots[sl][:, kc * 128:(kc + 1) * 128]
                t.matmul(psums[sl][:, 0:512], lhsT, rhs_ap(kc, 0),
                         start=(kc == 0), stop=(kc == KC - 1))
                inst = t.matmul(psums[sl][:, 512:1024], lhsT, rhs_ap(kc, 512),
                                start=(kc == 0), stop=(kc == KC - 1))
            k.inc(inst, k.sT)
        wfree[sl] = (k.sT, k.sT.cnt)
        _stage(k, base_stage)
        pfree[sl] = evac(idx, psums[sl])



def _qkv0(k):
    ap = k.ap

    @k.sec("s")
    def _(s):
        k.inc(s.dma_start(out=k.bias[:, 0:32], in_=ap["l0_bqk"]), k.dS)
        k.inc(s.dma_start(out=k.bias[:, 32:48], in_=ap["l0_bp"]), k.dS)
        k.inc(s.dma_start(out=k.bias[:, 48:112], in_=ap["l0_lnw"]), k.dS)
    bias_dS = k.dS.cnt

    @k.sec("g")
    def _(g):
        k.inc(g.dma_start(out=k.bbf[0:1, 0:2048], in_=ap["l0_bv"]), k.dG)
    bv_dG = k.dG.cnt

    def rhs(kc, n0):
        return k.P1[:, kc * T + n0:kc * T + n0 + 512]

    stage = [k.ATN[:, 0:1024], k.ATN[:, 1024:2048]]
    stage_free = [None, None]

    def evac(oc, ps):
        sl = oc % 2

        @k.sec("v")
        def _(v, oc=oc, sl=sl):
            _w(v, k.sT)
            if oc == 0:
                v.wait_ge(k.dS.h, bias_dS)
            if stage_free[sl] is not None:
                v.wait_ge(stage_free[sl][0].h, stage_free[sl][1])
            k.inc(v.tensor_scalar(stage[sl][:, :], ps[:, :], k.bias[:, oc:oc + 1],
                                  None, ALU.add), k.sV)
        fr = (k.sV, k.sV.cnt)

        @k.sec("s")
        def _(s, oc=oc, sl=sl):
            _w(s, k.sV)
            if oc < 16:
                last = s.dma_start(out=ap["q_hbm"][oc], in_=stage[sl][:, :])
            else:
                last = s.dma_start(out=ap["kv0k_in"][oc - 16], in_=stage[sl][:, :])
            k.inc(last, k.dS)
        stage_free[sl] = (k.dS, k.dS.cnt)
        return fr

    _proj_fm(k, 32, D, lambda oc: ap["l0_wqk"][oc], rhs, evac, queue="g",
             pre=[(k.sV, k.ln_vdone), (k.cC, k.cC.cnt)])

    for ng in range(4):
        @k.sec("g")
        def _(g, ng=ng):
            _w(g, k.sT)
            if ng == 0:
                _w(g, k.cC)
            for kq in range(4):
                src = ap["l0_wv"].rearrange("(c p) o -> p c o", p=128)[
                    :, kq * 4:(kq + 1) * 4, ng * 512:(ng + 1) * 512]
                k.inc(g.dma_start(
                    out=k.WSa[:, :].rearrange("p (c f) -> p c f", c=DC)[:, kq * 4:(kq + 1) * 4, :],
                    in_=src), k.dW)
        dv = k.dW.cnt

        for jt in range(NT):
            ps = k.ps_c if jt % 2 == 0 else k.ps_d

            @k.sec("t")
            def _(t, ng=ng, jt=jt, ps=ps, dv=dv):
                t.wait_ge(k.dW.h, dv)
                if ng == 0 and jt == 0:
                    t.wait_ge(k.dG.h, bv_dG)
                _w(t, k.sA)
                for kc in range(DC):
                    t.matmul(ps[:, 0:512], k.P1[:, kc * T + jt * 128:kc * T + (jt + 1) * 128],
                             k.WSa[:, kc * 512:(kc + 1) * 512], start=(kc == 0), stop=False)
                k.inc(t.matmul(ps[:, 0:512], k.C_ones_row, k.bbf[0:1, ng * 512:(ng + 1) * 512],
                               start=False, stop=True), k.sT)

            @k.sec("a")
            def _(a, ps=ps, jt=jt):
                _w(a, k.sT)
                _w(a, k.dS)
                k.inc(a.activation(k.ATN[:, 2048 + (jt % 2) * 512:2560 + (jt % 2) * 512],
                                   ps[:, :], AF.Copy), k.sA)

            @k.sec("s")
            def _(s, ng=ng, jt=jt):
                _w(s, k.sA)
                off = 2048 + (jt % 2) * 512
                for i in range(4):
                    k.inc(s.dma_start(
                        out=ap["kv0v_in"][ng * 4 + i, jt, :, :],
                        in_=k.ATN[:, off + i * 128:off + (i + 1) * 128]), k.dS)

    @k.sec("g")
    def _(g):
        g.wait_ge(k.dS.h, k.dS.cnt)
        pairs = [[0, 1], [2, 3], [4, 5], [6, 7]]
        k.inc(g.collective_compute("AllGather", ALU.bypass, replica_groups=pairs,
                                   ins=[ap["kv0k_in"]], outs=[ap["kv0k_out"]]), k.cC)
        k.inc(g.collective_compute("AllGather", ALU.bypass, replica_groups=pairs,
                                   ins=[ap["kv0v_in"]], outs=[ap["kv0v_out"]]), k.cC)
    k.kv_ready = k.cC.cnt



def _ffn_quant(k, blk):
    ap = k.ap
    HC = 64 if blk == 0 else 44
    mi_g, mi_u, mi_d = (0, 1, 2) if blk == 0 else (3, 4, 5)
    nm_g, nm_u, nm_d = (("t_l0g", "t_l0u", "t_l0d") if blk == 0
                        else ("t_l2a", "t_l2b", "t_l2c"))
    has_rms = blk == 0

    @k.sec("s")
    def _(s):
        _w(s, k.sT)
        s.wait_ge(k.dS.h, k.qx_dS)
        for q4 in range(4):
            k.inc(s.dma_start(
                out=k.P1[:, :].rearrange("p (c t) -> p c t", c=DC)[:, q4 * 4:(q4 + 1) * 4, :],
                in_=ap["qx_hbm"][q4 * 4:(q4 + 1) * 4].rearrange("c p t -> p c t")), k.dS)
    qx_dS = k.dS.cnt

    _cols_to_row(k, ST_INVS0, 8, k.rowA)
    _bcast_row(k, k.rowA, k.bcA)

    @k.sec("v")
    def _(v):
        _w(v, k.sV)
        v.wait_ge(k.sV.h, k.tern_scales_ready)
        if has_rms:
            v.tensor_scalar(k.lnw_eff[:, 0:64], k.bias[:, 48:112],
                            k.stat[:, ST_TERN + mi_u:ST_TERN + mi_u + 1], None, ALU.mult)
            v.tensor_mul(k.bcB[:, 0:64], k.bias[:, 48:112], k.bias[:, 48:112])
            v.reciprocal(k.bcB[:, 0:64], k.bcB[:, 0:64])
            inst = v.tensor_copy(k.ilnw2bf[:, 0:64], k.bcB[:, 0:64])
        else:
            inst = v.tensor_scalar(k.lnw_eff[:, 0:1],
                                   k.stat[:, ST_TERN + mi_u:ST_TERN + mi_u + 1], 1.0, None, ALU.mult)
        inst = v.tensor_scalar(k.bcB[:, :], k.bcA[:, :],
                               k.stat[:, ST_TERN + mi_g:ST_TERN + mi_g + 1], None, ALU.mult)
        k.inc(inst, k.sV)
    scales_v = k.sV.cnt

    tern_wait = max(k.tern_ready[nm_g], k.tern_ready[nm_u])
    for hc in range(HC):
        sl = hc % 2

        @k.sec("a")
        def _(a, hc=hc, sl=sl):
            if hc == 0:
                a.wait_ge(k.cC.h, tern_wait)
                a.wait_ge(k.sV.h, scales_v)
                a.wait_ge(k.dS.h, qx_dS)
            if hc >= 2:
                _w(a, k.sT)
            k.inc(a.dma_start(out=k.ATN[:, sl * 2048:sl * 2048 + 2048],
                              in_=_tern_stripe_ap(k, nm_g, hc)), k.dK)

        @k.sec("g")
        def _(g, hc=hc, sl=sl):
            if hc == 0:
                _w(g, k.cC)
            if hc >= 2:
                _w(g, k.sT)
            k.inc(g.dma_start(out=k.ATN[:, 4096 + sl * 2048:4096 + sl * 2048 + 2048],
                              in_=_tern_stripe_ap(k, nm_u, hc)), k.dW)
        dk_v, dw_v = k.dK.cnt, k.dW.cnt

        @k.sec("t")
        def _(t, hc=hc, sl=sl, dk_v=dk_v, dw_v=dw_v):
            t.wait_ge(k.dK.h, dk_v)
            t.wait_ge(k.dW.h, dw_v)
            _w(t, k.sV)
            if hc == 0:
                t.wait_ge(k.dS.h, qx_dS)
            gst = k.ATN[:, sl * 2048:sl * 2048 + 2048]
            ust = k.ATN[:, 4096 + sl * 2048:4096 + sl * 2048 + 2048]
            for kc in range(DC):
                t.matmul(k.ps_a[:, 0:512], gst[:, kc * 128:(kc + 1) * 128],
                         k.P1[:, kc * T:kc * T + 512], start=(kc == 0), stop=(kc == DC - 1))
                t.matmul(k.ps_a[:, 512:1024], gst[:, kc * 128:(kc + 1) * 128],
                         k.P1[:, kc * T + 512:kc * T + 1024], start=(kc == 0), stop=(kc == DC - 1))
            for kc in range(DC):
                t.matmul(k.ps_b[:, 0:512], ust[:, kc * 128:(kc + 1) * 128],
                         k.P1[:, kc * T:kc * T + 512], start=(kc == 0), stop=(kc == DC - 1))
                inst = t.matmul(k.ps_b[:, 512:1024], ust[:, kc * 128:(kc + 1) * 128],
                                k.P1[:, kc * T + 512:kc * T + 1024], start=(kc == 0), stop=(kc == DC - 1))
            k.inc(inst, k.sT)
        mm_t = k.sT.cnt

        @k.sec("v")
        def _(v, mm_t=mm_t):
            v.wait_ge(k.sT.h, mm_t)
            k.inc(v.tensor_mul(k.SCR[:, 0:1024], k.ps_a[:, :], k.bcB[:, :]), k.sV)

        @k.sec("a")
        def _(a):
            _w(a, k.sV)
            k.inc(a.activation(k.SCR[:, 1024:2048], k.SCR[:, 0:1024], AF.Silu), k.sA)

        @k.sec("v")
        def _(v, hc=hc):
            _w(v, k.sA)
            v.tensor_mul(k.SCR[:, 4096:5120], k.ps_b[:, :], k.bcA[:, :])
            lcol = k.lnw_eff[:, hc:hc + 1] if has_rms else k.lnw_eff[:, 0:1]
            k.inc(v.scalar_tensor_tensor(k.SCR[:, 2048:3072], k.SCR[:, 1024:2048], lcol,
                                         k.SCR[:, 4096:5120], ALU.mult, ALU.mult), k.sV)
        hn_v = k.sV.cnt

        if has_rms:
            @k.sec("a")
            def _(a, hn_v=hn_v):
                a.wait_ge(k.sV.h, hn_v)
                k.inc(a.activation(k.hsq[:, 0:1024], k.SCR[:, 2048:3072], AF.Square), k.sA)
            sq_a = k.sA.cnt

        @k.sec("t")
        def _(t, hc=hc, hn_v=hn_v):
            t.wait_ge(k.sV.h, hn_v)
            for i in range(8):
                inst = t.transpose(k.ps_a[:, i * 128:(i + 1) * 128],
                                   k.SCR[:, 2048 + i * 128:2048 + (i + 1) * 128], k.cident_f[:, :])
            if has_rms:
                t.wait_ge(k.sA.h, sq_a)
                t.matmul(k.ps_c[0:1, 0:512], k.ilnw2bf[:, hc:hc + 1], k.hsq[:, 0:512],
                         start=(hc == 0), stop=(hc == HC - 1))
                inst = t.matmul(k.ps_d[0:1, 0:512], k.ilnw2bf[:, hc:hc + 1], k.hsq[:, 512:1024],
                                start=(hc == 0), stop=(hc == HC - 1))
            k.inc(inst, k.sT)
        tr_t = k.sT.cnt

        @k.sec("v")
        def _(v, tr_t=tr_t):
            v.wait_ge(k.sT.h, tr_t)
            _w(v, k.dS)
            k.inc(v.tensor_copy(k.SCR[:, 5120:6144], k.ps_a[:, :]), k.sV)

        @k.sec("s")
        def _(s, hc=hc):
            _w(s, k.sV)
            for jt in range(NT):
                k.inc(s.dma_start(out=ap["h_tm_hbm"][jt, :, hc * 128:(hc + 1) * 128],
                                  in_=k.SCR[:, 5120 + jt * 128:5120 + (jt + 1) * 128]), k.dS)
    k.h_dS = k.dS.cnt

    if has_rms:
        @k.sec("v")
        def _(v):
            _w(v, k.sT)
            v.tensor_copy(k.rowB[0:1, 0:512], k.ps_c[0:1, 0:512])
            k.inc(v.tensor_copy(k.rowB[0:1, 512:1024], k.ps_d[0:1, 0:512]), k.sV)

        @k.sec("a")
        def _(a):
            _w(a, k.sV)
            k.inc(a.activation(k.rowB[0:1, 0:1024], k.rowB[0:1, 0:1024], AF.Sqrt,
                               scale=k.constp(1.0 / 8192.0, 1), bias=k.constp(1e-6, 1)), k.sA)

        @k.sec("v")
        def _(v):
            _w(v, k.sA)
            k.inc(v.reciprocal(k.rowB[0:1, 0:1024], k.rowB[0:1, 0:1024]), k.sV)
        for grp in range(2):
            @k.sec("t")
            def _(t, grp=grp):
                _w(t, k.sV)
                for i in range(4):
                    inst = t.transpose(k.ps_c[:, i:i + 1],
                                       k.rowB[0:1, (grp * 4 + i) * 128:(grp * 4 + i + 1) * 128],
                                       k.cident_f[0:1, 0:1])
                k.inc(inst, k.sT)

            @k.sec("v")
            def _(v, grp=grp):
                _w(v, k.sT)
                for i in range(4):
                    inst = v.tensor_copy(k.stat[:, ST_R3 + grp * 4 + i:ST_R3 + grp * 4 + i + 1],
                                         k.ps_c[:, i:i + 1])
                k.inc(inst, k.sV)

    def qhT_dst(c, jh):
        if c < 32:
            return k.P1[:, c * 512 + jh * 128:c * 512 + (jh + 1) * 128]
        if c < 48:
            return k.WSa[:, (c - 32) * 512 + jh * 128:(c - 32) * 512 + (jh + 1) * 128]
        return k.WSb[:, (c - 48) * 512 + jh * 128:(c - 48) * 512 + (jh + 1) * 128]

    swu_div = k.stat[:, ST_ABS + mi_u:ST_ABS + mi_u + 1]
    for half in range(2):
        for jh in range(4):
            jt = half * 4 + jh

            @k.sec("s")
            def _(s, jt=jt):
                _w(s, k.sV)
                _w(s, k.sT)
                s.wait_ge(k.dS.h, k.h_dS)
                k.inc(s.dma_start(out=k.SCR[:, 0:8192], in_=ap["h_tm_hbm"][jt]), k.dS)
            hload = k.dS.cnt

            @k.sec("v")
            def _(v, jt=jt, hload=hload):
                v.wait_ge(k.dS.h, hload)
                st = k.stat
                v.tensor_reduce(st[:, ST_T0:ST_T0 + 1], k.SCR[:, 0:8192], AX.X, ALU.max,
                                apply_absolute_value=True)
                if has_rms:
                    v.tensor_copy(st[:, ST_T1:ST_T1 + 1], st[:, ST_R3 + jt:ST_R3 + jt + 1])
                else:
                    v.memset(st[:, ST_T1:ST_T1 + 1], 1.0)
                v.tensor_mul(st[:, ST_T0:ST_T0 + 1], st[:, ST_T0:ST_T0 + 1], st[:, ST_T1:ST_T1 + 1])
                v.tensor_scalar(st[:, ST_T0:ST_T0 + 1], st[:, ST_T0:ST_T0 + 1], 1e-5, None, ALU.max)
                v.reciprocal(st[:, ST_T2:ST_T2 + 1], st[:, ST_T0:ST_T0 + 1])
                v.tensor_scalar(st[:, ST_T2:ST_T2 + 1], st[:, ST_T2:ST_T2 + 1], 127.0, None, ALU.mult)
                v.tensor_scalar(st[:, ST_INVS3 + jt:ST_INVS3 + jt + 1], st[:, ST_T0:ST_T0 + 1],
                                1.0 / 127.0, None, ALU.mult)
                v.tensor_mul(st[:, ST_T2:ST_T2 + 1], st[:, ST_T2:ST_T2 + 1], st[:, ST_T1:ST_T1 + 1])
                v.tensor_scalar(k.SCR[:, 0:8192], k.SCR[:, 0:8192], st[:, ST_T2:ST_T2 + 1],
                                RND_C, ALU.mult, ALU.add)
                k.inc(v.tensor_scalar(k.ATN[:, 0:8192], k.SCR[:, 0:8192],
                                      RND_C, None, ALU.subtract), k.sV)
            q_v = k.sV.cnt

            for rnd in range((HC + 7) // 8):
                nn = min(8, HC - rnd * 8)

                @k.sec("t")
                def _(t, rnd=rnd, q_v=q_v, nn=nn):
                    t.wait_ge(k.sV.h, q_v if rnd == 0 else k.sV.cnt)
                    for i in range(nn):
                        c = rnd * 8 + i
                        inst = t.transpose(k.ps_t[:, i * 128:(i + 1) * 128],
                                           k.ATN[:, c * 128:(c + 1) * 128], k.C_ident)
                    k.inc(inst, k.sT)

                @k.sec("v")
                def _(v, rnd=rnd, jh=jh, nn=nn):
                    _w(v, k.sT)
                    for i in range(nn):
                        c = rnd * 8 + i
                        inst = v.tensor_copy(qhT_dst(c, jh), k.ps_t[:, i * 128:(i + 1) * 128])
                    k.inc(inst, k.sV)

        _cols_to_row(k, ST_INVS3 + half * 4, 4, k.rowA)
        _bcast_row(k, k.rowA, k.bcB, n=512,
                   scale_col=k.stat[:, ST_TERN + mi_d:ST_TERN + mi_d + 1])
        tern_d = k.tern_ready[nm_d]
        for dc in range(16):
            @k.sec("g")
            def _(g, dc=dc, half=half):
                if dc == 0 and half == 0:
                    _w(g, k.cC)
                _w(g, k.sT)
                k.inc(g.dma_start(out=k.ATN[:, 0:HC * 128],
                                  in_=_tern_stripe_ap(k, nm_d, dc)[:, 0:HC * 128]), k.dW)
            ddv = k.dW.cnt

            @k.sec("t")
            def _(t, dc=dc, ddv=ddv):
                t.wait_ge(k.dW.h, ddv)
                _w(t, k.sV)
                for hc in range(HC):
                    inst = t.matmul(k.ps_c[:, 0:512], k.ATN[:, hc * 128:(hc + 1) * 128],
                                    _qhT_rhs(k, hc), start=(hc == 0), stop=(hc == HC - 1))
                k.inc(inst, k.sT)

            @k.sec("v")
            def _(v, dc=dc):
                _w(v, k.sT)
                _w(v, k.sA)
                k.inc(v.tensor_mul(k.SCR[:, 0:512], k.ps_c[:, :], k.bcB[:, 0:512]), k.sV)

            @k.sec("a")
            def _(a, dc=dc):
                _w(a, k.sV)
                _w(a, k.dS)
                k.inc(a.activation(k.hsq[:, 0:512], k.SCR[:, 0:512], AF.Silu), k.sA)

            @k.sec("s")
            def _(s, dc=dc, half=half):
                _w(s, k.sA)
                k.inc(s.dma_start(out=ap["g_hbm"][dc, :, half * 512:half * 512 + 512],
                                  in_=k.hsq[:, 0:512]), k.dS)
    k.g_ready = k.dS.cnt


def _qhT_rhs(k, hc):
    if hc < 32:
        return k.P1[:, hc * 512:(hc + 1) * 512]
    if hc < 48:
        return k.WSa[:, (hc - 32) * 512:(hc - 32 + 1) * 512]
    return k.WSb[:, (hc - 48) * 512:(hc - 48 + 1) * 512]





def _emit_pv(k, t, si, kind, u, c0, c1, eoff, kcol, ev, last_c):
    t.wait_ge(k.sV.h, ev[0])
    t.wait_ge(k.sA.h, ev[1])
    KH, VH = 0, 2048
    first = si == 0
    bank = "b0" if c0 < 512 else "b1"
    t.matmul(k.ps_b[:, c0:c1], k.ATN[:, VH + (kcol - KH):VH + (kcol - KH) + 128],
             k.ATN[:, eoff:eoff + (c1 - c0)],
             start=first, stop=(last_c[bank] == (si, c0)))
    tgt = k.ps_c if c0 < 512 else k.ps_d
    o0 = c0 if c0 < 512 else c0 - 512
    sb = "c" if c0 < 512 else "d"
    k.inc(t.matmul(tgt[0:1, o0:o0 + (c1 - c0)], k.C_ones_col,
                   k.ATN[:, eoff:eoff + (c1 - c0)],
                   start=first, stop=(last_c[sb] == (si, c0))), k.sT)

def _attn0(k):
    ap = k.ap
    KH, VH, QH, E0, E1 = 0, 2048, 4096, 5120, 6144

    kvw = k.kv_ready
    for hd in range(16):
        @k.sec("s")
        def _(s, hd=hd):
            if hd == 0:
                s.wait_ge(k.cC.h, kvw)
            _w(s, k.sT)
            k.inc(s.dma_start(out=k.ATN[:, KH:KH + 1024], in_=ap["kv0k_out"][0, hd]), k.dS)
            k.inc(s.dma_start(out=k.ATN[:, KH + 1024:KH + 2048], in_=ap["kv0k_out"][1, hd]), k.dS)
            k.inc(s.dma_start(out=k.ATN[:, VH:VH + 1024],
                              in_=ap["kv0v_out"][0, hd].rearrange("j p f -> p j f")), k.dS)
            k.inc(s.dma_start(out=k.ATN[:, VH + 1024:VH + 2048],
                              in_=ap["kv0v_out"][1, hd].rearrange("j p f -> p j f")), k.dS)
            k.inc(s.dma_start(out=k.ATN[:, QH:QH + 1024], in_=ap["q_hbm"][hd]), k.dS)
        ld_dS = k.dS.cnt

        for jt in range(NT):
            ps = k.ps_c if jt % 2 == 0 else k.ps_d
            for run, base in (("e", 0), ("o", 8)):
                for c0 in range(0, (jt + 1) * 128, 512):
                    c1 = min(c0 + 512, (jt + 1) * 128)

                    @k.sec("t")
                    def _(t, hd=hd, jt=jt, ps=ps, base=base, c0=c0, c1=c1, ld_dS=ld_dS):
                        t.wait_ge(k.dS.h, ld_dS)
                        _w(t, k.sV)
                        k.inc(t.matmul(ps[:, 0:c1 - c0],
                                       k.ATN[:, QH + jt * 128:QH + (jt + 1) * 128],
                                       k.ATN[:, KH + base * 128 + c0:KH + base * 128 + c1],
                                       start=True, stop=True), k.sT)

                    @k.sec("v")
                    def _(v, jt=jt, ps=ps, run=run, c0=c0, c1=c1):
                        _w(v, k.sT)
                        dcol = jt * 128
                        if c0 <= dcol < c1:
                            mt = k.M_l0MeM if run == "e" else k.M_l0MoM
                            v.tensor_add(ps[:, dcol - c0:dcol - c0 + 128],
                                         ps[:, dcol - c0:dcol - c0 + 128], mt)
                        v.tensor_reduce(k.stat[:, ST_T0:ST_T0 + 1], ps[:, 0:c1 - c0], AX.X, ALU.max)
                        mcol = k.stat[:, ST_M + jt:ST_M + jt + 1]
                        if run == "e" and c0 == 0:
                            k.inc(v.tensor_copy(mcol, k.stat[:, ST_T0:ST_T0 + 1]), k.sV)
                        else:
                            k.inc(v.tensor_max(mcol, mcol, k.stat[:, ST_T0:ST_T0 + 1]), k.sV)

        _cols_to_row(k, ST_M, 8, k.rowA)
        _bcast_row(k, k.rowA, k.bcA)
        mb_v = k.sV.cnt

        slot_list = [("e", u) for u in range(8)] + [("o", u) for u in range(8)]

        def chunks_of(u):
            out = []
            w0 = u * 128
            if w0 < 512:
                out.append((w0, 512))
            out.append((max(w0, 512), 1024))
            return out

        last_c = {}
        for si, (kind, u) in enumerate(slot_list):
            for (c0, c1) in chunks_of(u):
                last_c["c" if c0 < 512 else "d"] = (si, c0)
                last_c["b0" if c0 < 512 else "b1"] = (si, c0)
        pend = None
        for si, (kind, u) in enumerate(slot_list):
            kcol = KH + (0 if kind == "e" else 1024) + u * 128
            for (c0, c1) in chunks_of(u):
                eoff = (E0 if (si % 2 == 0) else E1) + (c0 - u * 128)

                @k.sec("t")
                def _(t, kcol=kcol, c0=c0, c1=c1, mb_v=mb_v, pend=pend):
                    if pend is not None:
                        _emit_pv(k, t, *pend)
                    t.wait_ge(k.sV.h, max(mb_v, k.sV.cnt))
                    k.inc(t.matmul(k.ps_a[:, 0:c1 - c0],
                                   k.ATN[:, kcol:kcol + 128],
                                   k.ATN[:, QH + c0:QH + c1],
                                   start=True, stop=True), k.sT)

                @k.sec("v")
                def _(v, kind=kind, u=u, c0=c0, c1=c1):
                    _w(v, k.sT)
                    v.tensor_sub(k.SCR[:, 0:c1 - c0], k.ps_a[:, 0:c1 - c0], k.bcA[:, c0:c1])
                    if c0 == u * 128:
                        mt = k.M_l0MeT if kind == "e" else k.M_l0MoT
                        k.inc(v.tensor_add(k.SCR[:, 0:128], k.SCR[:, 0:128], mt), k.sV)
                    else:
                        k.inc(v.engine_nop(), k.sV)

                @k.sec("a")
                def _(a, c0=c0, c1=c1, eoff=eoff):
                    _w(a, k.sV)
                    _w(a, k.sT)
                    k.inc(a.activation(k.ATN[:, eoff:eoff + (c1 - c0)],
                                       k.SCR[:, 0:c1 - c0], AF.Exp), k.sA)
                ev = (k.sV.cnt, k.sA.cnt)
                pend = (si, kind, u, c0, c1, eoff, kcol, ev, last_c)

        @k.sec("t")
        def _(t, pend=pend):
            _emit_pv(k, t, *pend)

        @k.sec("v")
        def _(v):
            _w(v, k.sT)
            v.tensor_copy(k.rowA[0:1, 0:512], k.ps_c[0:1, 0:512])
            v.tensor_copy(k.rowA[0:1, 512:1024], k.ps_d[0:1, 0:512])
            v.reciprocal(k.rowA[0:1, 0:1024], k.rowA[0:1, 0:1024])
            k.inc(v.tensor_scalar(k.rowA[0:1, 0:1024], k.rowA[0:1, 0:1024],
                                  float(D) ** -0.5, None, ALU.mult), k.sV)
        _bcast_row(k, k.rowA, k.bcB)

        @k.sec("v")
        def _(v, hd=hd):
            _w(v, k.sV)
            k.inc(v.tensor_mul(k.P1[:, hd * T:(hd + 1) * T], k.ps_b[:, :], k.bcB[:, :]), k.sV)
    k.attn_o_v = k.sV.cnt



def _combine(k, w_name, bias_col0, pre=(), rhs=None, slots=None):
    ap = k.ap

    if rhs is None:
        def rhs(kc, n0):
            return k.P1[:, kc * T + n0:kc * T + n0 + 512]

    gst = [k.ATN[:, 0:1024], k.ATN[:, 1024:2048]]
    gfree = [None, None]
    base_stage = getattr(k, "_cur_stage", "")

    def evac(oc, ps):
        sl = oc % 2

        _stage(k, base_stage + "_g")
        @k.sec("s")
        def _(s, oc=oc, sl=sl):
            if gfree[sl] is not None:
                s.wait_ge(gfree[sl][0].h, gfree[sl][1])
            if oc < 2:
                _w(s, k.sT)
            s.wait_ge(k.dS.h, k.g_ready)
            k.inc(s.dma_start(out=gst[sl][:, :], in_=ap["g_hbm"][oc]), k.dS)
        g_dS = k.dS.cnt

        _stage(k, base_stage + "_v1")
        @k.sec("v")
        def _(v, oc=oc, sl=sl, g_dS=g_dS):
            _w(v, k.sT)
            _w(v, k.sA)
            v.wait_ge(k.dS.h, g_dS)
            if bias_col0 is not None:
                k.inc(v.scalar_tensor_tensor(k.SCR[:, 0:1024], ps[:, :],
                                             k.bias[:, bias_col0 + oc:bias_col0 + oc + 1],
                                             gst[sl][:, :], ALU.add, ALU.mult), k.sV)
            else:
                k.inc(v.tensor_mul(k.SCR[:, 0:1024], ps[:, :], gst[sl][:, :]), k.sV)
        gfree[sl] = (k.sV, k.sV.cnt)
        fr = (k.sV, k.sV.cnt)

        _stage(k, base_stage + "_tr")
        @k.sec("t")
        def _(t, oc=oc):
            _w(t, k.sV)
            for i in range(4):
                t.transpose(k.ps_c[:, i * 128:(i + 1) * 128],
                            k.SCR[:, i * 128:(i + 1) * 128], k.cident_f[:, :])
            for i in range(4):
                inst = t.transpose(k.ps_d[:, i * 128:(i + 1) * 128],
                                   k.SCR[:, 512 + i * 128:512 + (i + 1) * 128], k.cident_f[:, :])
            k.inc(inst, k.sT)

        _stage(k, base_stage + "_v2")
        @k.sec("v")
        def _(v, oc=oc):
            _w(v, k.sT)
            xv = k.x_tm[:, :].rearrange("p (j d) -> p j d", j=NT)[:, 0:4, oc * 128:(oc + 1) * 128]
            v.tensor_add(xv, xv, k.ps_c[:, 0:512].rearrange("p (j d) -> p j d", j=4))
            xv = k.x_tm[:, :].rearrange("p (j d) -> p j d", j=NT)[:, 4:8, oc * 128:(oc + 1) * 128]
            k.inc(v.tensor_add(xv, xv, k.ps_d[:, 0:512].rearrange("p (j d) -> p j d", j=4)), k.sV)
        return fr

    _proj_fm(k, 16, D, lambda oc: ap[w_name][oc], rhs, evac, queue="g", pre=pre,
             slots=slots)
    k.block_done_v = k.sV.cnt



def _rhs_P1(k):
    return lambda kc, n0: k.P1[:, kc * T + n0:kc * T + n0 + 512]


def _qkv_rope(k, w_name, n_oc, qname, kname=None, elu=False, pre=()):
    ap = k.ap
    COS, SIN = 6144, 7168
    stage = [k.ATN[:, 0:1024], k.ATN[:, 1024:2048]]
    stage_free = [None, None]

    def evac(oc, ps):
        sl = oc % 2

        @k.sec("a")
        def _(a, oc=oc):
            _w(a, k.sT)
            if oc == 0:
                a.wait_ge(k.dS.h, k.tbl_dS)
            k.inc(a.activation(k.hsq[:, 0:1024], ps[:, :], AF.Copy), k.sA)

        @k.sec("t")
        def _(t):
            _w(t, k.sA)
            _w(t, k.sV)
            t.matmul(k.ps_c[:, 0:512], k.C_rot, k.hsq[:, 0:512], start=True, stop=True)
            k.inc(t.matmul(k.ps_d[:, 0:512], k.C_rot, k.hsq[:, 512:1024],
                           start=True, stop=True), k.sT)

        @k.sec("v")
        def _(v, oc=oc, sl=sl, ps=ps):
            _w(v, k.sT)
            _w(v, k.sA)
            if oc < 2:
                _w(v, k.dG)
            if stage_free[sl] is not None:
                v.wait_ge(stage_free[sl][0].h, stage_free[sl][1])
            v.tensor_mul(k.SCR[:, 0:1024], ps[:, :], k.SCR[:, COS:COS + 1024])
            v.tensor_mul(k.SCR[:, 1024:1536], k.ps_c[:, 0:512], k.SCR[:, SIN:SIN + 512])
            v.tensor_mul(k.SCR[:, 1536:2048], k.ps_d[:, 0:512], k.SCR[:, SIN + 512:SIN + 1024])
            if not elu:
                inst = v.tensor_add(stage[sl][:, :], k.SCR[:, 0:1024], k.SCR[:, 1024:2048])
            else:
                v.tensor_add(k.SCR[:, 2048:3072], k.SCR[:, 0:1024], k.SCR[:, 1024:2048])
                inst = v.tensor_scalar(k.SCR[:, 0:1024], k.SCR[:, 2048:3072],
                                       0.0, None, ALU.min)
            k.inc(inst, k.sV)
        fr = (k.sV, k.sV.cnt)

        if elu:
            @k.sec("a")
            def _(a):
                _w(a, k.sV)
                k.inc(a.activation(k.SCR[:, 1024:2048], k.SCR[:, 0:1024], AF.Exp), k.sA)

            @k.sec("v")
            def _(v, sl=sl):
                _w(v, k.sA)
                k.inc(v.scalar_tensor_tensor(stage[sl][:, :], k.SCR[:, 2048:3072], 0.0,
                                             k.SCR[:, 1024:2048], ALU.max, ALU.add), k.sV)

        @k.sec("s")
        def _(s, oc=oc, sl=sl):
            _w(s, k.sV)
            if kname is None or oc < 16:
                dst = ap[qname][oc]
            else:
                dst = ap[kname][oc - 16]
            k.inc(s.dma_start(out=dst, in_=stage[sl][:, :]), k.dS)
        stage_free[sl] = (k.dS, k.dS.cnt)
        return fr

    _proj_fm(k, n_oc, D, lambda oc: ap[w_name][oc], _rhs_P1(k), evac, queue="g", pre=pre)


def _v_tm(k, w_name, kv_name, bias_dG=None):
    ap = k.ap
    for ng in range(4):
        @k.sec("g")
        def _(g, ng=ng):
            _w(g, k.sT)
            if ng == 0:
                _w(g, k.cC)
            for kq in range(4):
                src = ap[w_name].rearrange("(c p) o -> p c o", p=128)[
                    :, kq * 4:(kq + 1) * 4, ng * 512:(ng + 1) * 512]
                k.inc(g.dma_start(
                    out=k.WSa[:, :].rearrange("p (c f) -> p c f", c=DC)[:, kq * 4:(kq + 1) * 4, :],
                    in_=src), k.dW)
        dv = k.dW.cnt

        for jt in range(NT):
            ps = k.ps_c if jt % 2 == 0 else k.ps_d

            @k.sec("t")
            def _(t, ng=ng, jt=jt, ps=ps, dv=dv):
                t.wait_ge(k.dW.h, dv)
                if ng == 0 and jt == 0 and bias_dG is not None:
                    t.wait_ge(k.dG.h, bias_dG)
                _w(t, k.sA)
                _w(t, k.sV)
                for kc in range(DC):
                    inst = t.matmul(ps[:, 0:512],
                                    k.P1[:, kc * T + jt * 128:kc * T + (jt + 1) * 128],
                                    k.WSa[:, kc * 512:(kc + 1) * 512], start=(kc == 0),
                                    stop=(bias_dG is None and kc == DC - 1))
                if bias_dG is not None:
                    inst = t.matmul(ps[:, 0:512], k.C_ones_row,
                                    k.bbf[0:1, ng * 512:(ng + 1) * 512], start=False, stop=True)
                k.inc(inst, k.sT)

            @k.sec("a")
            def _(a, ps=ps, jt=jt):
                _w(a, k.sT)
                _w(a, k.dS)
                k.inc(a.activation(k.ATN[:, 2048 + (jt % 2) * 512:2560 + (jt % 2) * 512],
                                   ps[:, :], AF.Copy), k.sA)

            @k.sec("s")
            def _(s, ng=ng, jt=jt):
                _w(s, k.sA)
                off = 2048 + (jt % 2) * 512
                for i in range(4):
                    k.inc(s.dma_start(
                        out=ap[kv_name][ng * 4 + i, jt, :, :],
                        in_=k.ATN[:, off + i * 128:off + (i + 1) * 128]), k.dS)



def _qkv1(k):
    ap = k.ap

    @k.sec("s")
    def _(s):
        _w(s, k.sV)
        k.inc(s.dma_start(out=k.SCR[:, 6144:7168], in_=ap["rope1_cos"]), k.dS)
        k.inc(s.dma_start(out=k.SCR[:, 7168:8192], in_=ap["rope1_sin"]), k.dS)
    k.tbl_dS = k.dS.cnt

    _qkv_rope(k, "l1_wqk", 32, "q_hbm", "kv1k_in",
              pre=[(k.sV, k.ln_vdone), (k.cC, k.cC.cnt)])
    _v_tm(k, "l1_wv", "kv1v_in")

    @k.sec("g")
    def _(g):
        g.wait_ge(k.dS.h, k.dS.cnt)
        pairs = [[0, 1], [2, 3], [4, 5], [6, 7]]
        k.inc(g.collective_compute("AllGather", ALU.bypass, replica_groups=pairs,
                                   ins=[ap["kv1k_in"]], outs=[ap["kv1k_out"]]), k.cC)
        k.inc(g.collective_compute("AllGather", ALU.bypass, replica_groups=pairs,
                                   ins=[ap["kv1v_in"]], outs=[ap["kv1v_out"]]), k.cC)
    k.kv1_ready = k.cC.cnt


def _h1_rhs(k, hc):
    if hc < 16:
        return k.WSa[:, hc * 512:(hc + 1) * 512]
    if hc < 32:
        return k.WSb[:, (hc - 16) * 512:(hc - 16 + 1) * 512]
    return k.ATN[:, (hc - 32) * 512:(hc - 32 + 1) * 512]


def _ffn1(k):
    ap = k.ap
    HC = 44

    @k.sec("s")
    def _(s):
        _w(s, k.sV)
        k.inc(s.dma_start(out=k.bias[:, 0:44], in_=ap["l1_sg"]), k.dS)
        k.inc(s.dma_start(out=k.bias[:, 44:88], in_=ap["l1_su"]), k.dS)
        k.inc(s.dma_start(out=k.bias[:, 88:104], in_=ap["l1_sd"]), k.dS)
    bias_dS = k.dS.cnt

    for hc in range(HC):
        sl = hc % 2
        gsl = k.ATN[:, sl * 2048:sl * 2048 + 2048]
        usl = k.ATN[:, 4096 + sl * 2048:4096 + sl * 2048 + 2048]

        @k.sec("v")
        def _(v, hc=hc, gsl=gsl, usl=usl):
            _w(v, k.sT)
            if hc < 2:
                _w(v, k.dG)
            v.memset(gsl, -128.0)
            k.inc(v.memset(usl, -128.0), k.sV)
        ms_v = k.sV.cnt

        @k.sec("g")
        def _(g, hc=hc, gsl=gsl, usl=usl, ms_v=ms_v):
            g.wait_ge(k.sV.h, ms_v)
            if hc == 0:
                _w(g, k.cC)
            k.inc(g.dma_start(out=gsl, in_=ap["l1_wqg"][hc], accum_op=ALU.add), k.dW)
            k.inc(g.dma_start(out=usl, in_=ap["l1_wqu"][hc], accum_op=ALU.add), k.dG)
        dw_v, dg_v = k.dW.cnt, k.dG.cnt

        @k.sec("t")
        def _(t, hc=hc, gsl=gsl, usl=usl, dw_v=dw_v, dg_v=dg_v):
            t.wait_ge(k.dW.h, dw_v)
            t.wait_ge(k.dG.h, dg_v)
            _w(t, k.sV)
            for kc in range(DC):
                t.matmul(k.ps_a[:, 0:512], gsl[:, kc * 128:(kc + 1) * 128],
                         k.P1[:, kc * T:kc * T + 512], start=(kc == 0), stop=(kc == DC - 1))
                t.matmul(k.ps_a[:, 512:1024], gsl[:, kc * 128:(kc + 1) * 128],
                         k.P1[:, kc * T + 512:kc * T + 1024], start=(kc == 0), stop=(kc == DC - 1))
            for kc in range(DC):
                t.matmul(k.ps_b[:, 0:512], usl[:, kc * 128:(kc + 1) * 128],
                         k.P1[:, kc * T:kc * T + 512], start=(kc == 0), stop=(kc == DC - 1))
                inst = t.matmul(k.ps_b[:, 512:1024], usl[:, kc * 128:(kc + 1) * 128],
                                k.P1[:, kc * T + 512:kc * T + 1024], start=(kc == 0),
                                stop=(kc == DC - 1))
            k.inc(inst, k.sT)
        mm_t = k.sT.cnt

        @k.sec("v")
        def _(v, hc=hc, mm_t=mm_t):
            v.wait_ge(k.sT.h, mm_t)
            if hc == 0:
                v.wait_ge(k.dS.h, bias_dS)
            k.inc(v.tensor_scalar(k.SCR[:, 0:1024], k.ps_a[:, :], k.bias[:, hc:hc + 1],
                                  None, ALU.mult), k.sV)

        @k.sec("a")
        def _(a):
            _w(a, k.sV)
            k.inc(a.activation(k.SCR[:, 1024:2048], k.SCR[:, 0:1024], AF.Silu), k.sA)

        @k.sec("v")
        def _(v, hc=hc):
            _w(v, k.sA)
            _w(v, k.dS)
            k.inc(v.scalar_tensor_tensor(k.hsq[:, 0:1024], k.ps_b[:, :],
                                         k.bias[:, 44 + hc:44 + hc + 1],
                                         k.SCR[:, 1024:2048], ALU.mult, ALU.mult), k.sV)

        @k.sec("s")
        def _(s, hc=hc):
            _w(s, k.sV)
            k.inc(s.dma_start(out=ap["hT_hbm"][hc], in_=k.hsq[:, 0:1024]), k.dS)
    h_dS = k.dS.cnt

    for half in range(2):
        @k.sec("a")
        def _(a, half=half):
            if half == 0:
                a.wait_ge(k.dS.h, h_dS)
            _w(a, k.sT)
            for hc in range(HC):
                k.inc(a.dma_start(out=_h1_rhs(k, hc),
                                  in_=ap["hT_hbm"][hc, :, half * 512:half * 512 + 512]), k.dK)
        hld = k.dK.cnt

        for dc in range(16):
            sl = dc % 2
            wsl = k.P1[:, sl * 5632:sl * 5632 + 5632]

            @k.sec("v")
            def _(v, dc=dc, wsl=wsl):
                _w(v, k.sT)
                k.inc(v.memset(wsl, -128.0), k.sV)
            ms_v = k.sV.cnt

            @k.sec("g")
            def _(g, dc=dc, wsl=wsl, ms_v=ms_v):
                g.wait_ge(k.sV.h, ms_v)
                for (o, wdt) in _pieces(5632):
                    k.inc(g.dma_start(out=wsl[:, o:o + wdt],
                                      in_=ap["l1_wqd"][dc, :, o:o + wdt],
                                      accum_op=ALU.add), k.dW)
            dw_v = k.dW.cnt

            ps = k.ps_c if dc % 2 == 0 else k.ps_d

            @k.sec("t")
            def _(t, dc=dc, wsl=wsl, dw_v=dw_v, ps=ps, hld=hld):
                t.wait_ge(k.dW.h, dw_v)
                t.wait_ge(k.dK.h, hld)
                _w(t, k.sV)
                for hc in range(HC):
                    inst = t.matmul(ps[:, 0:512], wsl[:, hc * 128:(hc + 1) * 128],
                                    _h1_rhs(k, hc), start=(hc == 0), stop=(hc == HC - 1))
                k.inc(inst, k.sT)

            @k.sec("v")
            def _(v, dc=dc, ps=ps):
                _w(v, k.sT)
                _w(v, k.sA)
                k.inc(v.tensor_scalar(k.SCR[:, 0:512], ps[:, :], k.bias[:, 88 + dc:89 + dc],
                                      None, ALU.mult), k.sV)

            @k.sec("a")
            def _(a):
                _w(a, k.sV)
                _w(a, k.dS)
                k.inc(a.activation(k.hsq[:, 0:512], k.SCR[:, 0:512], AF.Silu), k.sA)

            @k.sec("s")
            def _(s, dc=dc, half=half):
                _w(s, k.sA)
                k.inc(s.dma_start(out=ap["g_hbm"][dc, :, half * 512:half * 512 + 512],
                                  in_=k.hsq[:, 0:512]), k.dS)
    k.g_ready = k.dS.cnt


def _attn1(k):
    ap = k.ap
    KH, VH, QH, E0, E1 = 0, 2048, 4096, 5120, 6144

    @k.sec("s")
    def _(s):
        for i, nm in enumerate(("lambda_q1", "lambda_k1", "lambda_q2", "lambda_k2")):
            k.inc(s.dma_start(out=k.bias[:, 112 + i:113 + i], in_=ap["l1_" + nm]), k.dS)
    lam_dS = k.dS.cnt

    @k.sec("v")
    def _(v):
        v.wait_ge(k.dS.h, lam_dS)
        v.tensor_mul(k.stat[:, ST_T0:ST_T0 + 1], k.bias[:, 112:113], k.bias[:, 113:114])
        k.inc(v.tensor_mul(k.stat[:, ST_T1:ST_T1 + 1], k.bias[:, 114:115],
                           k.bias[:, 115:116]), k.sV)
    lam_v = k.sV.cnt

    @k.sec("t")
    def _(t, lam_v=lam_v):
        t.wait_ge(k.sV.h, lam_v)
        _w(t, k.sA)
        t.matmul(k.ps_c[0:1, 0:1], k.stat[:, ST_T0:ST_T0 + 1], k.C_onescol_f,
                 start=True, stop=True)
        k.inc(t.matmul(k.ps_c[0:1, 1:2], k.stat[:, ST_T1:ST_T1 + 1], k.C_onescol_f,
                       start=True, stop=True), k.sT)
    lam_t = k.sT.cnt

    @k.sec("a")
    def _(a, lam_t=lam_t):
        a.wait_ge(k.sT.h, lam_t)
        k.inc(a.activation(k.rowC[0:1, 0:2], k.ps_c[0:1, 0:2], AF.Exp), k.sA)

    @k.sec("v")
    def _(v):
        _w(v, k.sA)
        v.tensor_sub(k.rowC[0:1, 16:17], k.rowC[0:1, 0:1], k.rowC[0:1, 1:2])
        k.inc(v.tensor_scalar(k.rowC[0:1, 16:17], k.rowC[0:1, 16:17], 0.8, None, ALU.add), k.sV)

    kvw = k.kv1_ready
    for hd in range(16):
        _stage(k, "attn1_ld")
        @k.sec("s")
        def _(s, hd=hd):
            if hd == 0:
                s.wait_ge(k.cC.h, kvw)
            _w(s, k.sT)
            _w(s, k.sA)
            k.inc(s.dma_start(out=k.ATN[:, KH:KH + 1024], in_=ap["kv1k_out"][0, hd]), k.dS)
            k.inc(s.dma_start(out=k.ATN[:, KH + 1024:KH + 2048], in_=ap["kv1k_out"][1, hd]), k.dS)
            k.inc(s.dma_start(out=k.ATN[:, VH:VH + 1024],
                              in_=ap["kv1v_out"][0, hd].rearrange("j p f -> p j f")), k.dS)
            k.inc(s.dma_start(out=k.ATN[:, VH + 1024:VH + 2048],
                              in_=ap["kv1v_out"][1, hd].rearrange("j p f -> p j f")), k.dS)
            k.inc(s.dma_start(out=k.ATN[:, QH:QH + 1024], in_=ap["q_hbm"][hd]), k.dS)
        ld_dS = k.dS.cnt

        slot_list = [("e", u) for u in range(8)] + [("o", u) for u in range(8)]
        last_c = {}
        for si, (kind, u) in enumerate(slot_list):
            for c0 in (0, 512):
                last_c["c" if c0 < 512 else "d"] = (si, c0)
                last_c["b0" if c0 < 512 else "b1"] = (si, c0)

        for phase in range(2):
            pb = 64 * phase
            pend = None
            for si, (kind, u) in enumerate(slot_list):
                kcol = KH + (0 if kind == "e" else 1024) + u * 128
                for c0 in (0, 512):
                    c1 = c0 + 512
                    eoff = (E0 if si % 2 == 0 else E1) + c0
                    aw = (c0, min(u * 128, c1))
                    bw = (u * 128, u * 128 + 128) if c0 <= u * 128 < c1 else None
                    cw = (max((u + 1) * 128, c0), c1)

                    _stage(k, "attn1_sc")
                    @k.sec("t")
                    def _(t, pb=pb, kcol=kcol, c0=c0, c1=c1, pend=pend, ld_dS=ld_dS):
                        if pend is not None:
                            _emit_pv(k, t, *pend)
                        else:
                            t.wait_ge(k.dS.h, ld_dS)
                            _w(t, k.sV)
                        k.inc(t.matmul(k.ps_a[:, 0:512],
                                       k.ATN[pb:pb + 64, kcol:kcol + 128],
                                       k.ATN[pb:pb + 64, QH + c0:QH + c1],
                                       start=True, stop=True), k.sT)

                    _stage(k, "attn1_msk")
                    @k.sec("v")
                    def _(v, kind=kind, bw=bw, c0=c0):
                        _w(v, k.sT)
                        if bw is not None:
                            mt = k.M_l1Me1T if kind == "e" else k.M_l1Mo1T
                            k.inc(v.scalar_tensor_tensor(
                                k.SCR[:, 0:128], k.ps_a[:, bw[0] - c0:bw[0] - c0 + 128],
                                SC_L1, mt, ALU.mult, ALU.add), k.sV)
                        else:
                            k.inc(v.engine_nop(), k.sV)

                    _stage(k, "attn1_exp")
                    @k.sec("a")
                    def _(a, aw=aw, bw=bw, cw=cw, c0=c0, eoff=eoff):
                        _w(a, k.sV)
                        _w(a, k.sT)
                        inst = None
                        if aw[1] > aw[0]:
                            inst = a.activation(k.ATN[:, eoff + aw[0] - c0:eoff + aw[1] - c0],
                                                k.ps_a[:, aw[0] - c0:aw[1] - c0], AF.Exp,
                                                scale=k.constp(SC_L1), bias=1.0)
                        if bw is not None:
                            inst = a.activation(k.ATN[:, eoff + bw[0] - c0:eoff + bw[0] - c0 + 128],
                                                k.SCR[:, 0:128], AF.Exp)
                        if cw[1] > cw[0]:
                            inst = a.activation(k.ATN[:, eoff + cw[0] - c0:eoff + cw[1] - c0],
                                                k.ps_a[:, cw[0] - c0:cw[1] - c0], AF.Exp,
                                                scale=k.constp(SC_L1))
                        k.inc(inst, k.sA)
                    ev = (k.sV.cnt, k.sA.cnt)
                    pend = (si, kind, u, c0, c1, eoff, kcol, ev, last_c)

            _stage(k, "attn1_ev")
            @k.sec("t")
            def _(t, pend=pend):
                _emit_pv(k, t, *pend)

            @k.sec("v")
            def _(v, phase=phase):
                _w(v, k.sT)
                v.tensor_copy(k.rowA[0:1, 0:512], k.ps_c[0:1, 0:512])
                v.tensor_copy(k.rowA[0:1, 512:1024], k.ps_d[0:1, 0:512])
                v.reciprocal(k.rowA[0:1, 0:1024], k.rowA[0:1, 0:1024])
                if phase == 1:
                    inst = v.tensor_scalar(k.rowA[0:1, 0:1024], k.rowA[0:1, 0:1024],
                                           k.rowC[0:1, 16:17], None, ALU.mult)
                else:
                    inst = v.engine_nop()
                k.inc(inst, k.sV)
            _bcast_row(k, k.rowA, k.bcA if phase == 0 else k.bcB)

            @k.sec("v")
            def _(v, phase=phase, hd=hd):
                _w(v, k.sV)
                if phase == 0:
                    k.inc(v.tensor_mul(k.SCR[:, 4096:5120], k.ps_b[:, :], k.bcA[:, :]), k.sV)
                else:
                    v.tensor_mul(k.SCR[:, 5120:6144], k.ps_b[:, :], k.bcB[:, :])
                    k.inc(v.tensor_sub(k.P1[:, hd * T:(hd + 1) * T], k.SCR[:, 4096:5120],
                                       k.SCR[:, 5120:6144]), k.sV)
    k.attn1_o_v = k.sV.cnt
    if "p1b1" in DBG_TENSORS:
        @k.sec("s")
        def _(s):
            _w(s, k.sV)
            for hc in range(16):
                k.inc(s.dma_start(out=k.ap["dbg_p1pre"][hc],
                                  in_=k.P1[:, hc * T:(hc + 1) * T]), k.dS)
            k.inc(s.dma_start(out=k.ap["dbg_qe"], in_=ap["q_hbm"]), k.dS)

    _stage(k, "attn1_gn")
    for hc in range(16):
        @k.sec("a")
        def _(a, hc=hc):
            _w(a, k.sV)
            _w(a, k.sT)
            k.inc(a.activation(k.hsq[:, 0:1024], k.P1[:, hc * T:(hc + 1) * T], AF.Square), k.sA)

        @k.sec("t")
        def _(t, hc=hc):
            _w(t, k.sA)
            if hc == 0:
                _w(t, k.sV)
            st, sp = hc == 0, hc == 15
            t.matmul(k.ps_a[0:1, 0:512], k.C_ones_col, k.P1[:, hc * T:hc * T + 512], start=st, stop=sp)
            t.matmul(k.ps_a[0:1, 512:1024], k.C_ones_col, k.P1[:, hc * T + 512:hc * T + 1024],
                     start=st, stop=sp)
            t.matmul(k.ps_b[0:1, 0:512], k.C_ones_col, k.hsq[:, 0:512], start=st, stop=sp)
            k.inc(t.matmul(k.ps_b[0:1, 512:1024], k.C_ones_col, k.hsq[:, 512:1024],
                           start=st, stop=sp), k.sT)

    @k.sec("s")
    def _(s):
        _w(s, k.sT)
        k.inc(s.dma_start(out=k.rowA[0:1, :], in_=ap["l1_gam_row"]), k.dS)
        k.inc(s.dma_start(out=k.rowB[0:1, :], in_=ap["l1_bet_row"]), k.dS)
    gb_dS = k.dS.cnt

    @k.sec("v")
    def _(v):
        _w(v, k.sT)
        c = k.rowC
        v.tensor_reduce(c[0:1, 32:40], k.ps_a[0:1, 0:1024].rearrange("o (g t) -> o g t", g=8),
                        AX.X, ALU.add)
        v.tensor_reduce(c[0:1, 40:48], k.ps_b[0:1, 0:1024].rearrange("o (g t) -> o g t", g=8),
                        AX.X, ALU.add)
        v.tensor_scalar(c[0:1, 32:40], c[0:1, 32:40], 1.0 / 262144.0, None, ALU.mult)
        v.tensor_scalar(c[0:1, 40:48], c[0:1, 40:48], 1.0 / 262144.0, None, ALU.mult)
        v.tensor_mul(c[0:1, 48:56], c[0:1, 32:40], c[0:1, 32:40])
        k.inc(v.tensor_sub(c[0:1, 40:48], c[0:1, 40:48], c[0:1, 48:56]), k.sV)

    @k.sec("a")
    def _(a):
        _w(a, k.sV)
        k.inc(a.activation(k.rowC[0:1, 40:48], k.rowC[0:1, 40:48], AF.Sqrt,
                           bias=k.constp(1e-5, 1)), k.sA)

    @k.sec("v")
    def _(v, gb_dS=gb_dS):
        _w(v, k.sA)
        v.wait_ge(k.dS.h, gb_dS)
        c = k.rowC
        v.reciprocal(c[0:1, 40:48], c[0:1, 40:48])
        for gg in range(8):
            v.tensor_scalar(k.bcA[0:1, gg * 128:(gg + 1) * 128], k.C_onesrow_f,
                            0.0, c[0:1, 32 + gg:33 + gg], ALU.mult, ALU.add)
        for gg in range(8):
            v.tensor_scalar(k.bcB[0:1, gg * 128:(gg + 1) * 128], k.C_onesrow_f,
                            0.0, c[0:1, 40 + gg:41 + gg], ALU.mult, ALU.add)
        v.tensor_mul(k.rowA[0:1, :], k.rowA[0:1, :], k.bcB[0:1, 0:1024])
        v.tensor_scalar(k.rowA[0:1, :], k.rowA[0:1, :], 0.2, None, ALU.mult)
        v.tensor_mul(k.bcB[0:1, 0:1024], k.bcA[0:1, 0:1024], k.rowA[0:1, :])
        v.tensor_scalar(k.rowB[0:1, :], k.rowB[0:1, :], 0.2, None, ALU.mult)
        k.inc(v.tensor_sub(k.rowB[0:1, :], k.rowB[0:1, :], k.bcB[0:1, 0:1024]), k.sV)

    _bcast_row(k, k.rowA, k.bcA)
    _bcast_row(k, k.rowB, k.bcB)

    for hc in range(16):
        @k.sec("v")
        def _(v, hc=hc):
            _w(v, k.sV)
            v.tensor_mul(k.SCR[:, 0:1024], k.P1[:, hc * T:(hc + 1) * T], k.bcA[:, :])
            k.inc(v.tensor_add(k.P1[:, hc * T:(hc + 1) * T], k.SCR[:, 0:1024],
                               k.bcB[:, :]), k.sV)
    k.attn1_gn_v = k.sV.cnt



def _qkv2(k):
    ap = k.ap

    stage = [k.ATN[:, 0:1024], k.ATN[:, 1024:2048]]
    stage_free = [None, None]

    def evac(oc, ps):
        sl = oc % 2

        @k.sec("a")
        def _(a, oc=oc, sl=sl, ps=ps):
            _w(a, k.sT)
            if stage_free[sl] is not None:
                a.wait_ge(stage_free[sl][0].h, stage_free[sl][1])
            k.inc(a.activation(stage[sl][:, :], ps[:, :], AF.Copy), k.sA)
        fr = (k.sA, k.sA.cnt)

        @k.sec("s")
        def _(s, oc=oc, sl=sl):
            _w(s, k.sA)
            k.inc(s.dma_start(out=ap["q_hbm"][oc], in_=stage[sl][:, :]), k.dS)
        stage_free[sl] = (k.dS, k.dS.cnt)
        return fr

    _proj_fm(k, 16, D, lambda oc: ap["l2_wv"][oc], _rhs_P1(k), evac, queue="g",
             pre=[(k.sV, k.ln_vdone)])
    k.v2_dS = k.dS.cnt

    for wi, w_name in enumerate(("l2_wq", "l2_wk")):
        for ng in range(4):
            @k.sec("g")
            def _(g, w_name=w_name, ng=ng):
                _w(g, k.sT)
                _w(g, k.sA)
                _w(g, k.dS)
                _w(g, k.cC)
                for kq in range(4):
                    src = ap[w_name].rearrange("(c p) o -> p c o", p=128)[
                        :, kq * 4:(kq + 1) * 4, ng * 512:(ng + 1) * 512]
                    k.inc(g.dma_start(
                        out=k.ATN[:, :].rearrange("p (c f) -> p c f", c=DC)[
                            :, kq * 4:(kq + 1) * 4, :],
                        in_=src), k.dW)
            dv = k.dW.cnt

            for jt in range(NT):
                ps = k.ps_c if jt % 2 == 0 else k.ps_d

                @k.sec("t")
                def _(t, ng=ng, jt=jt, ps=ps, dv=dv, wi=wi):
                    t.wait_ge(k.dW.h, dv)
                    _w(t, k.sA)
                    for kc in range(DC):
                        inst = t.matmul(ps[:, 0:512],
                                        k.P1[:, kc * T + jt * 128:kc * T + (jt + 1) * 128],
                                        k.ATN[:, kc * 512:(kc + 1) * 512],
                                        start=(kc == 0), stop=(kc == DC - 1))
                    k.inc(inst, k.sT)

                @k.sec("a")
                def _(a, wi=wi, ng=ng, jt=jt, ps=ps):
                    _w(a, k.sT)
                    if wi == 0:
                        _w(a, k.dS)
                        dst = k.hsq[:, (jt % 2) * 512:(jt % 2) * 512 + 512]
                    else:
                        ws = k.WSa if jt < 4 else k.WSb
                        dst = ws[:, (jt % 4) * 2048 + ng * 512:(jt % 4) * 2048 + (ng + 1) * 512]
                    k.inc(a.activation(dst, ps[:, :], AF.Copy), k.sA)

                if wi == 0:
                    @k.sec("s")
                    def _(s, ng=ng, jt=jt):
                        _w(s, k.sA)
                        k.inc(s.dma_start(out=ap["kv2q_in"][jt, :, ng * 512:(ng + 1) * 512],
                                          in_=k.hsq[:, (jt % 2) * 512:(jt % 2) * 512 + 512]),
                              k.dS)

    k.q2st_dS = k.dS.cnt
    @k.sec("s")
    def _(s):
        _w(s, k.sT)
        s.wait_ge(k.dS.h, k.q2st_dS)
        for jt in range(NT):
            k.inc(s.dma_start(out=k.P1[:, jt * D:(jt + 1) * D], in_=ap["kv2q_in"][jt]), k.dS)
    q_dS = k.dS.cnt

    for ft in range(16):
        @k.sec("t")
        def _(t, ft=ft, q_dS=q_dS):
            if ft == 0:
                t.wait_ge(k.dS.h, q_dS)
            _w(t, k.sV)
            for jt in range(NT):
                ws = k.WSa if jt < 4 else k.WSb
                lhsT = ws[:, (jt % 4) * 2048 + ft * 128:(jt % 4) * 2048 + (ft + 1) * 128]
                for nc_ in range(2):
                    t.matmul(k.ps_a[:, nc_ * 512:(nc_ + 1) * 512], lhsT,
                             k.P1[:, jt * D + nc_ * 512:jt * D + (nc_ + 1) * 512],
                             start=(jt == 0), stop=(jt == NT - 1))
                for nc_ in range(2):
                    inst = t.matmul(k.ps_b[:, nc_ * 512:(nc_ + 1) * 512], lhsT,
                                    k.P1[:, jt * D + 1024 + nc_ * 512:jt * D + 1024 + (nc_ + 1) * 512],
                                    start=(jt == 0), stop=(jt == NT - 1))
            k.inc(inst, k.sT)
        mm_t = k.sT.cnt

        @k.sec("v")
        def _(v, ft=ft, mm_t=mm_t):
            v.wait_ge(k.sT.h, mm_t)
            _w(v, k.dS)
            v.tensor_copy(k.SCR[:, 0:1024], k.ps_a[:, :])
            k.inc(v.tensor_copy(k.SCR[:, 1024:2048], k.ps_b[:, :]), k.sV)

        @k.sec("s")
        def _(s, ft=ft):
            _w(s, k.sV)
            k.inc(s.dma_start(out=ap["s2_in"][ft], in_=k.SCR[:, 0:2048]), k.dS)

    @k.sec("g")
    def _(g):
        g.wait_ge(k.dS.h, k.dS.cnt)
        pairs = [[0, 1], [2, 3], [4, 5], [6, 7]]
        k.inc(g.collective_compute("AllReduce", ALU.add, replica_groups=pairs,
                                   ins=[ap["s2_in"]], outs=[ap["s2_out"]]), k.cC)
    k.s2_ready = k.cC.cnt


def _attn2(k):
    ap = k.ap

    def wsdst(ec):
        ws = k.WSa if ec < 8 else k.WSb
        return ws, (ec % 8) * 1024

    for eh in range(2):
        for ft in range(16):
            @k.sec("s")
            def _(s, eh=eh, ft=ft):
                if eh == 0 and ft == 0:
                    s.wait_ge(k.cC.h, k.s2_ready)
                _w(s, k.sA)
                _w(s, k.sT)
                k.inc(s.dma_start(out=k.SCR[:, (ft % 4) * 1024:(ft % 4) * 1024 + 1024],
                                  in_=ap["s2_out"][ft, :, eh * 1024:eh * 1024 + 1024]), k.dS)
            sld = k.dS.cnt

            @k.sec("a")
            def _(a, ft=ft, sld=sld):
                a.wait_ge(k.dS.h, sld)
                _w(a, k.sT)
                k.inc(a.activation(k.P1[:, ft * T:(ft + 1) * T],
                                   k.SCR[:, (ft % 4) * 1024:(ft % 4) * 1024 + 1024],
                                   AF.Exp, scale=k.constp(0.125)), k.sA)

            @k.sec("t")
            def _(t, ft=ft):
                _w(t, k.sA)
                st, sp = ft == 0, ft == 15
                t.matmul(k.ps_c[0:1, 0:512], k.C_ones_col, k.P1[:, ft * T:ft * T + 512],
                         start=st, stop=sp)
                k.inc(t.matmul(k.ps_d[0:1, 0:512], k.C_ones_col,
                               k.P1[:, ft * T + 512:(ft + 1) * T], start=st, stop=sp), k.sT)

        @k.sec("v")
        def _(v):
            _w(v, k.sT)
            v.tensor_copy(k.rowA[0:1, 0:512], k.ps_c[0:1, 0:512])
            v.tensor_copy(k.rowA[0:1, 512:1024], k.ps_d[0:1, 0:512])
            k.inc(v.reciprocal(k.rowA[0:1, 0:1024], k.rowA[0:1, 0:1024]), k.sV)
        _bcast_row(k, k.rowA, k.bcA)

        for st in range(NT):
            @k.sec("s")
            def _(s, st=st, eh=eh):
                _w(s, k.sT)
                s.wait_ge(k.dS.h, k.v2_dS)
                for hh in range(2):
                    k.inc(s.dma_start(
                        out=k.ATN[:, hh * 1024:(hh + 1) * 1024].rearrange(
                            "p (c f) -> p c f", c=8),
                        in_=ap["q_hbm"][hh * 8:(hh + 1) * 8, :, st * 128:(st + 1) * 128]
                        .rearrange("c p s -> p c s")), k.dS)
            vld = k.dS.cnt
            ps = k.ps_a if st % 2 == 0 else k.ps_b

            @k.sec("t")
            def _(t, st=st, ps=ps, vld=vld):
                t.wait_ge(k.dS.h, vld)
                _w(t, k.sV)
                for ft in range(16):
                    for nc_ in range(2):
                        inst = t.matmul(ps[:, nc_ * 512:(nc_ + 1) * 512],
                                        k.ATN[:, ft * 128:(ft + 1) * 128],
                                        k.P1[:, ft * T + nc_ * 512:ft * T + (nc_ + 1) * 512],
                                        start=(ft == 0), stop=(ft == 15))
                k.inc(inst, k.sT)
            pv_t = k.sT.cnt

            @k.sec("v")
            def _(v, ps=ps, pv_t=pv_t):
                v.wait_ge(k.sT.h, pv_t)
                _w(v, k.sA)
                _w(v, k.dS)
                k.inc(v.tensor_mul(k.hsq[:, 0:1024], ps[:, :], k.bcA[:, :]), k.sV)
            ovt = k.sV.cnt

            @k.sec("t")
            def _(t, ovt=ovt):
                t.wait_ge(k.sV.h, ovt)
                for i in range(8):
                    inst = t.transpose(k.ps_t[:, i * 128:(i + 1) * 128],
                                       k.hsq[:, i * 128:(i + 1) * 128], k.C_ident)
                k.inc(inst, k.sT)

            @k.sec("v")
            def _(v, eh=eh, st=st):
                _w(v, k.sT)
                for i in range(8):
                    ws, off = wsdst(eh * 8 + i)
                    inst = v.tensor_copy(ws[:, off + st * 128:off + (st + 1) * 128],
                                         k.ps_t[:, i * 128:(i + 1) * 128])
                k.inc(inst, k.sV)
    k.attn2_o_v = k.sV.cnt



def _reshard(k):
    ap = k.ap

    @k.sec("s")
    def _(s):
        _w(s, k.sV)
        for j in range(NT):
            k.inc(s.dma_start(out=ap["resh_in"][j], in_=k.x_tm[:, j * D:(j + 1) * D]), k.dS)
    xs_dS = k.dS.cnt

    @k.sec("g")
    def _(g, xs_dS=xs_dS):
        g.wait_ge(k.dS.h, xs_dS)
        pairs = [[0, 1], [2, 3], [4, 5], [6, 7]]
        k.inc(g.collective_compute("AllGather", ALU.bypass, replica_groups=pairs,
                                   ins=[ap["resh_in"]], outs=[ap["resh_out"]]), k.cC)
    rs_c = k.cC.cnt

    @k.sec("g")
    def _(g, rs_c=rs_c):
        g.wait_ge(k.cC.h, rs_c)
        src = ap["resh_out"].rearrange("w j p f -> (w j p) f")
        for i in range(NT):
            k.inc(g.indirect_dma_start(
                out=k.x_tm[:, i * D:(i + 1) * D], in_=src,
                in_offset=bass.IndirectOffsetOnAxis(ap=k.ridx[:, i:i + 1], axis=0),
                out_offset=None), k.dG)


def _qkv3(k):
    ap = k.ap

    @k.sec("s")
    def _(s):
        _w(s, k.sV)
        k.inc(s.dma_start(out=k.SCR[:, 6144:7168], in_=ap["rope3_cos"]), k.dS)
        k.inc(s.dma_start(out=k.SCR[:, 7168:8192], in_=ap["rope3_sin"]), k.dS)
        k.inc(s.dma_start(out=k.SCR[:, 4096:5120], in_=ap["rope3c_tm"]), k.dS)
        k.inc(s.dma_start(out=k.SCR[:, 5120:6144], in_=ap["rope3s_tm"]), k.dS)
        k.inc(s.dma_start(out=k.bias[:, 0:64], in_=ap["l3_bsyn"]), k.dS)
        k.inc(s.dma_start(out=k.bias[:, 64:80], in_=ap["l3_bout"]), k.dS)
        k.inc(s.dma_start(out=k.bias[:, 80:96], in_=ap["l3_gam"]), k.dS)
        k.inc(s.dma_start(out=k.bias[:, 96:112], in_=ap["l3_bet"]), k.dS)
    k.tbl_dS = k.dS.cnt

    _qkv_rope(k, "l3_wq", 16, "q_hbm", elu=True, pre=[(k.sV, k.ln_vdone)])
    k.q3_dS = k.dS.cnt

    for wi, w_name in enumerate(("l3_wk", "l3_wv")):
        @k.sec("g")
        def _(g, w_name=w_name):
            _w(g, k.sT)
            _w(g, k.cC)
            for kq in range(4):
                src = ap[w_name].rearrange("(c p) o -> p c o", p=128)[:, kq * 4:(kq + 1) * 4, :]
                k.inc(g.dma_start(
                    out=k.WSa[:, :].rearrange("p (c f) -> p c f", c=DC)[:, kq * 4:(kq + 1) * 4, :],
                    in_=src), k.dW)
        dv = k.dW.cnt

        for jt in range(NT):
            ps = k.ps_c if jt % 2 == 0 else k.ps_d

            @k.sec("t")
            def _(t, jt=jt, ps=ps, dv=dv):
                t.wait_ge(k.dW.h, dv)
                _w(t, k.sA)
                _w(t, k.sV)
                for kc in range(DC):
                    inst = t.matmul(ps[:, 0:512],
                                    k.P1[:, kc * T + jt * 128:kc * T + (jt + 1) * 128],
                                    k.WSa[:, kc * 512:(kc + 1) * 512],
                                    start=(kc == 0), stop=(kc == DC - 1))
                k.inc(inst, k.sT)

            if wi == 1:
                @k.sec("a")
                def _(a, jt=jt, ps=ps):
                    _w(a, k.sT)
                    k.inc(a.activation(k.WSb[:, 4096 + jt * 512:4096 + (jt + 1) * 512],
                                       ps[:, :], AF.Copy), k.sA)
            else:
                for kvh in range(4):
                    @k.sec("v")
                    def _(v, jt=jt, ps=ps, kvh=kvh):
                        _w(v, k.sT)
                        pt = ps[:, kvh * 128:(kvh + 1) * 128]
                        ct = k.SCR[:, 4096 + jt * 128:4096 + (jt + 1) * 128]
                        st_ = k.SCR[:, 5120 + jt * 128:5120 + (jt + 1) * 128]
                        v.tensor_mul(k.SCR[:, 0:128], pt, ct)
                        v.tensor_mul(k.SCR[:, 128:192], pt[:, 64:128], st_[:, 0:64])
                        v.tensor_mul(k.SCR[:, 192:256], pt[:, 0:64], st_[:, 64:128])
                        v.tensor_add(k.SCR[:, 256:384], k.SCR[:, 0:128], k.SCR[:, 128:256])
                        k.inc(v.tensor_scalar(k.SCR[:, 0:128], k.SCR[:, 256:384],
                                              0.0, None, ALU.min), k.sV)

                    @k.sec("a")
                    def _(a):
                        _w(a, k.sV)
                        k.inc(a.activation(k.SCR[:, 128:256], k.SCR[:, 0:128], AF.Exp), k.sA)

                    @k.sec("v")
                    def _(v, jt=jt, kvh=kvh):
                        _w(v, k.sA)
                        k.inc(v.scalar_tensor_tensor(
                            k.WSb[:, jt * 512 + kvh * 128:jt * 512 + (kvh + 1) * 128],
                            k.SCR[:, 256:384], 0.0, k.SCR[:, 128:256],
                            ALU.max, ALU.add), k.sV)

    for jt in range(NT):
        @k.sec("t")
        def _(t, jt=jt):
            _w(t, k.sV)
            for kvh in range(4):
                inst = t.transpose(k.ps_t[:, kvh * 128:(kvh + 1) * 128],
                                   k.WSb[:, jt * 512 + kvh * 128:jt * 512 + (kvh + 1) * 128],
                                   k.C_ident)
            k.inc(inst, k.sT)

        @k.sec("v")
        def _(v, jt=jt):
            _w(v, k.sT)
            for kvh in range(4):
                inst = v.tensor_copy(k.ATN[:, kvh * 1024 + jt * 128:kvh * 1024 + (jt + 1) * 128],
                                     k.ps_t[:, kvh * 128:(kvh + 1) * 128])
            k.inc(inst, k.sV)
    k.kfm_v = k.sV.cnt


def _attn3(k):
    ap = k.ap

    def kblk(c, kvh):
        return k.WSb[:, c * 512 + kvh * 128:c * 512 + (kvh + 1) * 128]

    def vblk(c, kvh):
        return k.WSb[:, 4096 + c * 512 + kvh * 128:4096 + c * 512 + (kvh + 1) * 128]

    @k.sec("t")
    def _(t):
        _w(t, k.sV)
        for kvh in range(4):
            for c in range(NT):
                t.matmul(k.ps_c[:, kvh * 128:(kvh + 1) * 128], kblk(c, kvh), vblk(c, kvh),
                         start=(c == 0), stop=(c == NT - 1))
                inst = t.matmul(k.ps_d[:, kvh:kvh + 1], kblk(c, kvh), k.C_ones_col,
                                start=(c == 0), stop=(c == NT - 1))
        k.inc(inst, k.sT)

    @k.sec("v")
    def _(v):
        _w(v, k.sT)
        for kvh in range(4):
            v.tensor_scalar(k.SCR[:, 2048 + kvh * 128:2048 + (kvh + 1) * 128],
                            k.ps_c[:, kvh * 128:(kvh + 1) * 128], k.C_role0, None, ALU.mult)
        k.inc(v.tensor_scalar(k.SCR[:, 2560:2564], k.ps_d[:, 0:4], k.C_role0,
                              None, ALU.mult), k.sV)

    @k.sec("s")
    def _(s):
        _w(s, k.sV)
        k.inc(s.dma_start(out=ap["st3_in"], in_=k.SCR[:, 2048:2564]), k.dS)
    st_dS = k.dS.cnt

    @k.sec("g")
    def _(g, st_dS=st_dS):
        g.wait_ge(k.dS.h, st_dS)
        pairs = [[0, 1], [2, 3], [4, 5], [6, 7]]
        k.inc(g.collective_compute("AllReduce", ALU.add, replica_groups=pairs,
                                   ins=[ap["st3_in"]], outs=[ap["st3_out"]]), k.cC)
    ar_c = k.cC.cnt

    @k.sec("s")
    def _(s, ar_c=ar_c):
        s.wait_ge(k.cC.h, ar_c)
        _w(s, k.sV)
        k.inc(s.dma_start(out=k.SCR[:, 2048:2564], in_=ap["st3_out"]), k.dS)
    sa_dS = k.dS.cnt

    for kvh in range(4):
        @k.sec("v")
        def _(v, kvh=kvh, sa_dS=sa_dS):
            v.wait_ge(k.dS.h, sa_dS)
            _w(v, k.sT)
            v.tensor_scalar(k.ATN[:, 5120:5248], k.SCR[:, 2048 + kvh * 128:2048 + (kvh + 1) * 128],
                            k.C_role1, None, ALU.mult)
            k.inc(v.tensor_scalar(k.cbf[:, 648:649], k.SCR[:, 2560 + kvh:2561 + kvh],
                                  k.C_role1, None, ALU.mult), k.sV)

        for c in range(NT):
            @k.sec("t")
            def _(t, c=c, kvh=kvh):
                _w(t, k.sV)
                t.matmul(k.ps_c[:, 0:128], kblk(c, kvh), vblk(c, kvh), start=True, stop=True)
                k.inc(t.matmul(k.ps_d[:, 0:1], kblk(c, kvh), k.C_ones_col,
                               start=True, stop=True), k.sT)

            @k.sec("v")
            def _(v, c=c):
                _w(v, k.sT)
                if c == 0:
                    v.tensor_copy(k.SCR[:, 0:128], k.ps_c[:, 0:128])
                    v.tensor_copy(k.stat[:, ST_T0:ST_T0 + 1], k.ps_d[:, 0:1])
                else:
                    v.tensor_add(k.SCR[:, 0:128], k.SCR[:, 0:128], k.ps_c[:, 0:128])
                    v.tensor_add(k.stat[:, ST_T0:ST_T0 + 1], k.stat[:, ST_T0:ST_T0 + 1],
                                 k.ps_d[:, 0:1])
                v.tensor_copy(k.ATN[:, 4096 + c * 128:4096 + (c + 1) * 128], k.SCR[:, 0:128])
                k.inc(v.tensor_copy(k.cbf[:, 640 + c:641 + c],
                                    k.stat[:, ST_T0:ST_T0 + 1]), k.sV)
        stA_v = k.sV.cnt

        for qh in range(4):
            hd = kvh * 4 + qh

            @k.sec("s")
            def _(s, hd=hd, stA_v=stA_v):
                s.wait_ge(k.sV.h, stA_v)
                _w(s, k.sT)
                s.wait_ge(k.dS.h, k.q3_dS)
                k.inc(s.dma_start(out=k.hsq[:, 0:1024], in_=ap["q_hbm"][hd]), k.dS)
            q_dS = k.dS.cnt

            for c in range(NT):
                @k.sec("t")
                def _(t, c=c, kvh=kvh, q_dS=q_dS):
                    if c == 0:
                        t.wait_ge(k.dS.h, q_dS)
                    _w(t, k.sV)
                    qb = k.hsq[:, c * 128:(c + 1) * 128]
                    k.inc(t.matmul(k.ps_d[:, 0:128],
                                   k.ATN[:, kvh * 1024 + c * 128:kvh * 1024 + (c + 1) * 128],
                                   qb, start=True, stop=True), k.sT)

                @k.sec("v")
                def _(v, c=c):
                    _w(v, k.sT)
                    k.inc(v.tensor_mul(k.ATN[:, 5248:5376], k.ps_d[:, 0:128],
                                       k.C_l3mask), k.sV)
                ms_v = k.sV.cnt

                @k.sec("t")
                def _(t, c=c, kvh=kvh, ms_v=ms_v):
                    t.wait_ge(k.sV.h, ms_v)
                    qb = k.hsq[:, c * 128:(c + 1) * 128]
                    if c > 0:
                        t.matmul(k.ps_a[:, c * 128:(c + 1) * 128],
                                 k.ATN[:, 4096 + (c - 1) * 128:4096 + c * 128], qb,
                                 start=True, stop=False)
                        t.matmul(k.ps_b[0:1, c * 128:(c + 1) * 128],
                                 k.cbf[:, 640 + c - 1:641 + c - 1], qb, start=True, stop=False)
                    t.matmul(k.ps_a[:, c * 128:(c + 1) * 128], vblk(c, kvh),
                             k.ATN[:, 5248:5376], start=(c == 0), stop=False)
                    t.matmul(k.ps_b[0:1, c * 128:(c + 1) * 128], k.C_ones_col,
                             k.ATN[:, 5248:5376], start=(c == 0), stop=False)
                    t.matmul(k.ps_a[:, c * 128:(c + 1) * 128], k.ATN[:, 5120:5248],
                             qb, start=False, stop=True)
                    k.inc(t.matmul(k.ps_b[0:1, c * 128:(c + 1) * 128], k.cbf[:, 648:649],
                                   qb, start=False, stop=True), k.sT)
            fin_t = k.sT.cnt

            @k.sec("v")
            def _(v, fin_t=fin_t):
                v.wait_ge(k.sT.h, fin_t)
                v.tensor_copy(k.SCR[:, 1024:2048], k.ps_a[:, :])
                if "nd3" in DBG_TENSORS:
                    v.tensor_copy(k.rowB[0:1, 0:1024], k.ps_b[0:1, 0:1024])
                v.tensor_scalar(k.rowA[0:1, 0:1024], k.ps_b[0:1, 0:1024], 1e-6, None, ALU.add)
                k.inc(v.reciprocal(k.rowA[0:1, 0:1024], k.rowA[0:1, 0:1024]), k.sV)

            if "nd3" in DBG_TENSORS:
                @k.sec("s")
                def _(s, hd=hd):
                    _w(s, k.sV)
                    k.inc(s.dma_start(out=k.ap["dbg_num3"][hd], in_=k.SCR[:, 1024:2048]), k.dS)
                    k.inc(s.dma_start(out=k.ap["dbg_den3"][hd], in_=k.rowB[0:1, 0:1024]), k.dS)
            _bcast_row(k, k.rowA, k.bcA)

            @k.sec("v")
            def _(v):
                _w(v, k.sV)
                k.inc(v.tensor_mul(k.hsq[:, 0:1024], k.SCR[:, 1024:2048], k.bcA[:, :]), k.sV)
            o_v = k.sV.cnt

            @k.sec("s")
            def _(s, hd=hd, o_v=o_v):
                s.wait_ge(k.sV.h, o_v)
                k.inc(s.dma_start(out=ap["kv1k_in"][hd], in_=k.hsq[:, 0:1024]), k.dS)
    k.o3_dS = k.dS.cnt


def _topk_tile(k, zld):
    st = k.stat
    LO, HI, CHI, MID, NMID, CV, CA, PM, T1_, T2_ = (ST_B + i for i in range(10))

    @k.sec("v")
    def _(v, zld=zld):
        v.wait_ge(k.dS.h, zld)
        _w(v, k.sA)
        _w(v, k.sT)
        v.tensor_reduce(st[:, HI:HI + 1], k.SCR[:, 0:8192], AX.X, ALU.max,
                        apply_absolute_value=True)
        v.tensor_scalar(st[:, LO:LO + 1], st[:, HI:HI + 1], -1.0000001, None, ALU.mult)
        v.tensor_scalar(st[:, HI:HI + 1], st[:, HI:HI + 1], 1.0000001, 1e-20, ALU.mult, ALU.add)
        k.inc(v.memset(st[:, CHI:CHI + 1], 0.0), k.sV)

    for it in range(15):
        @k.sec("v")
        def _(v, it=it):
            _w(v, k.sA)
            v.tensor_add(st[:, MID:MID + 1], st[:, LO:LO + 1], st[:, HI:HI + 1])
            v.tensor_scalar(st[:, MID:MID + 1], st[:, MID:MID + 1], 0.5, None, ALU.mult)
            v.tensor_scalar(st[:, NMID:NMID + 1], st[:, MID:MID + 1], -1.0, None, ALU.mult)
            k.inc(v.tensor_scalar(k.ATN[:, 0:4096], k.SCR[:, 0:4096], st[:, MID:MID + 1],
                                  0.0, ALU.is_ge, ALU.add, accum_out=st[:, CV:CV + 1]), k.sV)
        cv_v = k.sV.cnt

        _b = getattr(k, "_cur_stage", "")
        _stage(k, _b[:_b.rfind("_")] + "_tka" if _b.endswith("_tk") or _b.endswith("_tka") else _b)
        @k.sec("a")
        def _(a, cv_v=cv_v, it=it):
            a.wait_ge(k.sV.h, cv_v)
            if it == 0:
                _w(a, k.sT)
            k.inc(a.activation(k.ATN[:, 4096:8192], k.SCR[:, 4096:8192], AF.Sign,
                               bias=st[:, NMID:NMID + 1], accum_out=st[:, CA:CA + 1]), k.sA)
        _stage(k, _b)

        @k.sec("v")
        def _(v):
            _w(v, k.sA)
            v.tensor_scalar(st[:, CA:CA + 1], st[:, CA:CA + 1], 0.5, 2048.0, ALU.mult, ALU.add)
            v.tensor_add(st[:, CV:CV + 1], st[:, CV:CV + 1], st[:, CA:CA + 1])
            v.tensor_scalar(st[:, PM:PM + 1], st[:, CV:CV + 1], 819.0, None, ALU.is_ge)
            v.tensor_scalar(st[:, T1_:T1_ + 1], st[:, PM:PM + 1], -1.0, 1.0, ALU.mult, ALU.add)
            v.tensor_scalar(st[:, T2_:T2_ + 1], st[:, MID:MID + 1], st[:, PM:PM + 1], None, ALU.mult)
            v.tensor_scalar(st[:, LO:LO + 1], st[:, LO:LO + 1], st[:, T1_:T1_ + 1], None, ALU.mult)
            v.tensor_add(st[:, LO:LO + 1], st[:, LO:LO + 1], st[:, T2_:T2_ + 1])
            v.tensor_scalar(st[:, T2_:T2_ + 1], st[:, MID:MID + 1], st[:, T1_:T1_ + 1], None, ALU.mult)
            v.tensor_scalar(st[:, HI:HI + 1], st[:, HI:HI + 1], st[:, PM:PM + 1], None, ALU.mult)
            v.tensor_add(st[:, HI:HI + 1], st[:, HI:HI + 1], st[:, T2_:T2_ + 1])
            v.tensor_scalar(st[:, T2_:T2_ + 1], st[:, CV:CV + 1], st[:, T1_:T1_ + 1], None, ALU.mult)
            v.tensor_scalar(st[:, CHI:CHI + 1], st[:, CHI:CHI + 1], st[:, PM:PM + 1], None, ALU.mult)
            k.inc(v.tensor_add(st[:, CHI:CHI + 1], st[:, CHI:CHI + 1], st[:, T2_:T2_ + 1]), k.sV)

    _b2 = getattr(k, "_cur_stage", "")
    _stage(k, _b2 + "e")
    for c8 in range(8):
        @k.sec("v")
        def _(v, c8=c8):
            zc = k.SCR[:, c8 * 1024:(c8 + 1) * 1024]
            v.tensor_scalar(k.bcA[:, :], zc, st[:, HI:HI + 1], 3.0e38, ALU.is_ge, ALU.mult)
            v.tensor_sub(k.bcB[:, :], zc, k.bcA[:, :])
            v.max(k.msk[:, 640 + c8 * 16:640 + c8 * 16 + 8], k.bcB[:, :])
            v.match_replace(k.bcA[:, :], k.msk[:, 640 + c8 * 16:640 + c8 * 16 + 8],
                            k.bcB[:, :], -3.0e38)
            k.inc(v.max(k.msk[:, 640 + c8 * 16 + 8:640 + c8 * 16 + 16], k.bcA[:, :]), k.sV)

    _stage(k, _b2 + "f")
    @k.sec("v")
    def _(v):
        v.tensor_scalar(k.ATN[:, 0:4096], k.SCR[:, 0:4096], st[:, HI:HI + 1],
                        0.0, ALU.is_ge, ALU.add, accum_out=st[:, CHI:CHI + 1])
        v.tensor_scalar(k.ATN[:, 4096:8192], k.SCR[:, 4096:8192], st[:, HI:HI + 1],
                        0.0, ALU.is_ge, ALU.add, accum_out=st[:, CA:CA + 1])
        v.tensor_add(st[:, CHI:CHI + 1], st[:, CHI:CHI + 1], st[:, CA:CA + 1])
        v.max(k.msk[:, 0:8], k.msk[:, 640:768])
        v.match_replace(k.msk[:, 128:256], k.msk[:, 0:8], k.msk[:, 640:768], -3.0e38)
        v.max(k.msk[:, 8:16], k.msk[:, 128:256])
        v.tensor_scalar(st[:, T1_:T1_ + 1], st[:, CHI:CHI + 1], -1.0, 818.0, ALU.mult, ALU.add)
        v.tensor_scalar(k.msk[:, 16:32], k.C_iota16, st[:, T1_:T1_ + 1], None, ALU.is_equal)
        v.tensor_mul(k.msk[:, 16:32], k.msk[:, 16:32], k.msk[:, 0:16])
        v.tensor_reduce(st[:, T2_:T2_ + 1], k.msk[:, 16:32], AX.X, ALU.add)
        v.tensor_scalar(st[:, PM:PM + 1], st[:, T1_:T1_ + 1], 0.0, None, ALU.is_ge)
        v.tensor_mul(st[:, T2_:T2_ + 1], st[:, T2_:T2_ + 1], st[:, PM:PM + 1])
        v.tensor_scalar(st[:, PM:PM + 1], st[:, PM:PM + 1], -1.0, 1.0, ALU.mult, ALU.add)
        v.tensor_scalar(st[:, PM:PM + 1], st[:, HI:HI + 1], st[:, PM:PM + 1], None, ALU.mult)
        v.tensor_add(st[:, T2_:T2_ + 1], st[:, T2_:T2_ + 1], st[:, PM:PM + 1])
        v.scalar_tensor_tensor(k.ATN[:, 0:8192], k.SCR[:, 0:8192], st[:, T2_:T2_ + 1],
                               k.SCR[:, 0:8192], ALU.is_ge, ALU.mult)
        k.inc(v.tensor_scalar(k.ATN[:, 0:8192], k.ATN[:, 0:8192], 0.0, None, ALU.max), k.sV)


def _ffn3(k, tern_meta):
    ap = k.ap
    HC = 64
    base3 = getattr(k, "_cur_stage", "ffn3")

    _stage(k, base3 + "_syn")
    tern_s = k.tern_ready["t_l3s"]
    mult_s = k.stat[:, ST_TERN + 6:ST_TERN + 7]
    for hc in range(HC):
        sl = hc % 2

        @k.sec("a")
        def _(a, hc=hc, sl=sl):
            if hc == 0:
                a.wait_ge(k.cC.h, tern_s)
                _w(a, k.sV)
            _w(a, k.sT)
            k.inc(a.dma_start(out=k.ATN[:, sl * 2048:sl * 2048 + 2048],
                              in_=_tern_stripe_ap(k, "t_l3s", hc)), k.dK)
        dk_v = k.dK.cnt

        @k.sec("t")
        def _(t, hc=hc, sl=sl, dk_v=dk_v):
            t.wait_ge(k.dK.h, dk_v)
            _w(t, k.sV)
            gst = k.ATN[:, sl * 2048:sl * 2048 + 2048]
            for kc in range(DC):
                t.matmul(k.ps_a[:, 0:512], gst[:, kc * 128:(kc + 1) * 128],
                         k.P1[:, kc * T:kc * T + 512], start=(kc == 0), stop=(kc == DC - 1))
                inst = t.matmul(k.ps_a[:, 512:1024], gst[:, kc * 128:(kc + 1) * 128],
                                k.P1[:, kc * T + 512:kc * T + 1024], start=(kc == 0),
                                stop=(kc == DC - 1))
            k.inc(inst, k.sT)
        mm_t = k.sT.cnt

        @k.sec("v")
        def _(v, hc=hc, mm_t=mm_t):
            v.wait_ge(k.sT.h, mm_t)
            k.inc(v.tensor_scalar(k.SCR[:, 0:1024], k.ps_a[:, :], mult_s,
                                  k.bias[:, hc:hc + 1], ALU.mult, ALU.add), k.sV)
        z_v = k.sV.cnt

        @k.sec("t")
        def _(t, z_v=z_v):
            t.wait_ge(k.sV.h, z_v)
            for i in range(8):
                inst = t.transpose(k.ps_b[:, i * 128:(i + 1) * 128],
                                   k.SCR[:, i * 128:(i + 1) * 128], k.cident_f[:, :])
            k.inc(inst, k.sT)
        tr_t = k.sT.cnt

        @k.sec("v")
        def _(v, tr_t=tr_t):
            v.wait_ge(k.sT.h, tr_t)
            _w(v, k.dS)
            k.inc(v.tensor_copy(k.SCR[:, 1024:2048], k.ps_b[:, :]), k.sV)

        @k.sec("s")
        def _(s, hc=hc):
            _w(s, k.sV)
            for jt in range(NT):
                k.inc(s.dma_start(out=ap["h_tm_hbm"][jt, :, hc * 128:(hc + 1) * 128],
                                  in_=k.SCR[:, 1024 + jt * 128:1024 + (jt + 1) * 128]), k.dS)
    k.h_dS = k.dS.cnt

    mult_o = k.stat[:, ST_TERN + 7:ST_TERN + 8]
    tern_o = k.tern_ready["t_l3o"]
    for half in range(2):
        _stage(k, base3 + "_tk")
        for jh in range(4):
            jt = half * 4 + jh

            @k.sec("s")
            def _(s, jt=jt):
                _w(s, k.sV)
                _w(s, k.sT)
                _w(s, k.sA)
                s.wait_ge(k.dS.h, k.h_dS)
                k.inc(s.dma_start(out=k.SCR[:, 0:8192], in_=ap["h_tm_hbm"][jt]), k.dS)
            zld = k.dS.cnt
            _topk_tile(k, zld)
            _stage(k, base3 + "_tkt")
            q_v = k.sV.cnt
            if "hid3" in DBG_TENSORS:
                @k.sec("s")
                def _(s, jt=jt, q_v=q_v):
                    s.wait_ge(k.sV.h, q_v)
                    k.inc(s.dma_start(out=k.ap["dbg_hid3"][jt], in_=k.ATN[:, 0:8192]), k.dS)
            _stage(k, base3 + "_tk")

            for rnd in range(8):
                @k.sec("t")
                def _(t, rnd=rnd, q_v=q_v):
                    t.wait_ge(k.sV.h, q_v if rnd == 0 else k.sV.cnt)
                    for i in range(8):
                        c = rnd * 8 + i
                        inst = t.transpose(k.ps_t[:, i * 128:(i + 1) * 128],
                                           k.ATN[:, c * 128:(c + 1) * 128], k.C_ident)
                    k.inc(inst, k.sT)

                @k.sec("v")
                def _(v, rnd=rnd, jh=jh):
                    _w(v, k.sT)
                    for i in range(8):
                        c = rnd * 8 + i
                        inst = v.tensor_copy(_qhT_dst(k, c, jh), k.ps_t[:, i * 128:(i + 1) * 128])
                    k.inc(inst, k.sV)

        _stage(k, base3 + "_dn")
        for dc in range(16):
            @k.sec("g")
            def _(g, dc=dc, half=half):
                if dc == 0 and half == 0:
                    _w(g, k.cC)
                _w(g, k.sT)
                k.inc(g.dma_start(out=k.ATN[:, 0:8192], in_=_tern_stripe_ap(k, "t_l3o", dc)),
                      k.dW)
            ddv = k.dW.cnt

            @k.sec("s")
            def _(s, dc=dc, half=half):
                _w(s, k.sA)
                _w(s, k.sT)
                s.wait_ge(k.dS.h, k.qx_dS)
                k.inc(s.dma_start(out=k.hsq[:, 0:512],
                                  in_=ap["qx_hbm"][dc, :, half * 512:half * 512 + 512]), k.dS)
            xld = k.dS.cnt

            ps = k.ps_c if dc % 2 == 0 else k.ps_d

            @k.sec("t")
            def _(t, dc=dc, ddv=ddv, ps=ps):
                t.wait_ge(k.dW.h, ddv)
                _w(t, k.sV)
                for hc in range(HC):
                    inst = t.matmul(ps[:, 0:512], k.ATN[:, hc * 128:(hc + 1) * 128],
                                    _qhT_rhs(k, hc), start=(hc == 0), stop=(hc == HC - 1))
                k.inc(inst, k.sT)

            @k.sec("v")
            def _(v, dc=dc, ps=ps, xld=xld):
                _w(v, k.sT)
                v.wait_ge(k.dS.h, xld)
                sl = k.SCR[:, dc * 512:(dc + 1) * 512]
                v.tensor_scalar(sl, ps[:, :], mult_o, k.bias[:, 64 + dc:65 + dc],
                                ALU.mult, ALU.add)
                k.inc(v.tensor_add(sl, sl, k.hsq[:, 0:512]), k.sV)
            v_v = k.sV.cnt

            @k.sec("a")
            def _(a, dc=dc, v_v=v_v):
                a.wait_ge(k.sV.h, v_v)
                _w(a, k.sT)
                sl = k.SCR[:, dc * 512:(dc + 1) * 512]
                a.activation(k.hsq[:, 0:512], sl, AF.Copy)
                k.inc(a.activation(k.hsq[:, 512:1024], sl, AF.Square), k.sA)

            @k.sec("t")
            def _(t, dc=dc):
                _w(t, k.sA)
                st_, sp = dc == 0, dc == 15
                t.matmul(k.ps_a[0:1, 0:512], k.C_ones_col, k.hsq[:, 0:512], start=st_, stop=sp)
                k.inc(t.matmul(k.ps_b[0:1, 0:512], k.C_ones_col, k.hsq[:, 512:1024],
                               start=st_, stop=sp), k.sT)

        _stage(k, base3 + "_ln")
        @k.sec("v")
        def _(v):
            _w(v, k.sT)
            v.tensor_scalar(k.rowA[0:1, 0:512], k.ps_a[0:1, 0:512], 1.0 / D, None, ALU.mult)
            v.tensor_scalar(k.rowB[0:1, 0:512], k.ps_b[0:1, 0:512], 1.0 / D, None, ALU.mult)
            v.tensor_mul(k.rowB[0:1, 512:1024], k.rowA[0:1, 0:512], k.rowA[0:1, 0:512])
            k.inc(v.tensor_sub(k.rowB[0:1, 0:512], k.rowB[0:1, 0:512],
                               k.rowB[0:1, 512:1024]), k.sV)

        @k.sec("a")
        def _(a):
            _w(a, k.sV)
            k.inc(a.activation(k.rowB[0:1, 0:512], k.rowB[0:1, 0:512], AF.Sqrt,
                               bias=k.constp(1e-5, 1)), k.sA)

        @k.sec("v")
        def _(v):
            _w(v, k.sA)
            k.inc(v.reciprocal(k.rowB[0:1, 0:512], k.rowB[0:1, 0:512]), k.sV)
        _bcast_row(k, k.rowA, k.bcA, n=512)
        _bcast_row(k, k.rowB, k.bcB, n=512)

        for dc in range(16):
            @k.sec("v")
            def _(v, dc=dc):
                _w(v, k.sV)
                _w(v, k.sA)
                sl = k.SCR[:, dc * 512:(dc + 1) * 512]
                v.tensor_sub(sl, sl, k.bcA[:, 0:512])
                v.tensor_mul(sl, sl, k.bcB[:, 0:512])
                k.inc(v.tensor_scalar(sl, sl, k.bias[:, 80 + dc:81 + dc],
                                      k.bias[:, 96 + dc:97 + dc], ALU.mult, ALU.add), k.sV)

            @k.sec("a")
            def _(a, dc=dc):
                _w(a, k.sV)
                _w(a, k.dS)
                _w(a, k.sT)
                k.inc(a.activation(k.hsq[:, 0:512], k.SCR[:, dc * 512:(dc + 1) * 512],
                                   AF.Silu), k.sA)

            @k.sec("s")
            def _(s, dc=dc, half=half):
                _w(s, k.sA)
                k.inc(s.dma_start(out=ap["g_hbm"][dc, :, half * 512:half * 512 + 512],
                                  in_=k.hsq[:, 0:512]), k.dS)
    k.g_ready = k.dS.cnt

    @k.sec("s")
    def _(s):
        _w(s, k.sT)
        s.wait_ge(k.dS.h, k.o3_dS)
        for hd in range(16):
            k.inc(s.dma_start(out=k.P1[:, hd * T:(hd + 1) * T], in_=ap["kv1k_in"][hd]), k.dS)
    k.o3p1_dS = k.dS.cnt


def _qhT_dst(k, c, jh):
    if c < 32:
        return k.P1[:, c * 512 + jh * 128:c * 512 + (jh + 1) * 128]
    if c < 48:
        return k.WSa[:, (c - 32) * 512 + jh * 128:(c - 32) * 512 + (jh + 1) * 128]
    return k.WSb[:, (c - 48) * 512 + jh * 128:(c - 48) * 512 + (jh + 1) * 128]


def _stage(k, name):
    k._cur_stage = name
    k._safe_now = SAFE_MODE or (name in SAFE_STAGES)


def emit_all(k, nblocks, tern_meta):
    k.tern_meta = tern_meta
    _stage(k, "start"); _startup_loads(k)
    _stage(k, "tern"); _tern_phase1(k, tern_meta)
    _tern_phase2(k, tern_meta, ["t_l0g", "t_l0u", "t_l0d"])
    _stage(k, "ln"); _ln_quant(k, want_qx=True)
    _stage(k, "qkv"); _qkv0(k)
    _stage(k, "ffn"); _ffn_quant(k, 0)
    _stage(k, "attn"); _attn0(k)
    _stage(k, "comb"); _combine(k, "l0_wp", 32, pre=[(k.sV, k.attn_o_v)])
    if nblocks > 1:
        _stage(k, "tern2"); _tern_phase2(k, tern_meta, ["t_l2a", "t_l2b", "t_l2c",
                                                        "t_l3s", "t_l3o"])
        _stage(k, "ln1"); _ln_quant(k, want_qx=False)
        _stage(k, "qkv1"); _qkv1(k)
        _stage(k, "ffn1"); _ffn1(k)
        _stage(k, "attn1"); _attn1(k)
        if "p1b1" in DBG_TENSORS:
            k.dbg_p1pre_pending = True
        if "p1b1" in DBG_TENSORS:
            @k.sec("s")
            def _(s):
                _w(s, k.sV)
                for hc in range(16):
                    k.inc(s.dma_start(out=k.ap["dbg_p1b1"][hc],
                                      in_=k.P1[:, hc * T:(hc + 1) * T]), k.dS)
        _stage(k, "comb1"); _combine(k, "l1_wo", None, pre=[(k.sV, k.attn1_gn_v)])
    if nblocks > 2:
        _stage(k, "ln2"); _ln_quant(k, want_qx=True)
        _stage(k, "qkv2"); _qkv2(k)
        _stage(k, "ffn2"); _ffn_quant(k, 2)
        _stage(k, "attn2"); _attn2(k)
        if "o2" in DBG_TENSORS:
            @k.sec("s")
            def _(s):
                _w(s, k.sV)
                for ec in range(16):
                    ws = k.WSa if ec < 8 else k.WSb
                    k.inc(s.dma_start(out=k.ap["dbg_o2"][ec],
                                      in_=ws[:, (ec % 8) * 1024:(ec % 8) * 1024 + 1024]), k.dS)

        def rhs2(kc, n0):
            ws = k.WSa if kc < 8 else k.WSb
            return ws[:, (kc % 8) * 1024 + n0:(kc % 8) * 1024 + n0 + 512]
        _stage(k, "comb2"); _combine(k, "l2_wo", None, pre=[(k.sV, k.attn2_o_v)], rhs=rhs2,
                                      slots=[k.P1[:, 0:2048], k.P1[:, 2048:4096]])
    if nblocks > 3:
        _stage(k, "resh"); _reshard(k)
        _stage(k, "ln3"); _ln_quant(k, want_qx=False, save_xn=True)
        _stage(k, "qkv3"); _qkv3(k)
        _stage(k, "attn3"); _attn3(k)
        _stage(k, "ffn3"); _ffn3(k, tern_meta)
        _stage(k, "comb3"); _combine(k, "l3_wo", None, pre=[(k.dS, k.o3p1_dS)])

    vfin = k.sV.cnt

    @k.sec("s")
    def _(s):
        s.wait_ge(k.sV.h, vfin)
        for j in range(NT):
            k.inc(s.dma_start(out=k.ap["xout"][j * 128:(j + 1) * 128, :],
                              in_=k.x_tm[:, j * D:(j + 1) * D]), k.dS)
        for nm in DBG_COPIES:
            k.inc(s.dma_start(out=k.ap["dbg_" + nm], in_=k.ap[nm]), k.dS)
        if "row" in DBG_TENSORS:
            k.inc(s.dma_start(out=k.ap["dbg_row"][0:1, 0:256], in_=k.rowC[0:1, :]), k.dS)
            k.inc(s.dma_start(out=k.ap["dbg_row"][:, 256:352], in_=k.stat[:, :]), k.dS)
            k.inc(s.dma_start(out=k.ap["dbg_row"][0:1, 352:1376], in_=k.rowA[0:1, :]), k.dS)
        s.wait_ge(k.dS.h, k.dS.cnt)


_LAST_RES = None


def kernel(**inputs):
    global _LAST_RES
    in_maps, tern_meta = _prep_host(inputs)
    nc = build_program(NBLOCKS, tern_meta)
    res = run_bass_kernel_spmd(nc, in_maps, core_ids=list(range(8)))
    _LAST_RES = res
    out = np.empty((4, 2048, 2048), np.float32)
    for c in range(8):
        p, h = c // 2, c % 2
        if NBLOCKS >= 4:
            out[p, h * 1024:(h + 1) * 1024, :] = res.results[c]["xout"]
        else:
            r = res.results[c]["xout"]
            for j in range(NT):
                g = 2 * j + h
                out[p, g * 128:(g + 1) * 128, :] = r[j * 128:(j + 1) * 128, :]
    return out
